# version 06
# baseline (speedup 1.0000x reference)
import numpy as np
import ml_dtypes
import concourse.bass as bass
import concourse.mybir as mybir
from concourse.bass_utils import run_bass_kernel_spmd

F32 = mybir.dt.float32
BF16 = mybir.dt.bfloat16
AF = mybir.ActivationFunctionType
ALU = mybir.AluOpType
AX = mybir.AxisListType

C23 = 12582912.0
C52 = 6755399441055744.0
D = 1024
H = 2048
NEXP = 8
NH = 16
TPB = 128
TB = 16
NBATCH = 2
NCORE = 8

CAP = 768
DUMP = 6144.0
YOFF = 4160.0
SENT = 2048.0

RMS_EPS = 1e-6
Q_EPS = 1e-5
W_EPS = 1e-5
LN_EPS = 1e-5


class Ctr:
    def __init__(self, h):
        self.h = h
        self.n = 0


class _CutDone(Exception):
    pass


def build_program(debug=False, stop=None):
    nc = bass.Bass(target_bir_lowering=False, trn_type="TRN2", detect_race_conditions=False)

    x_in = nc.dram_tensor("x", [NBATCH * 2048, D], F32, kind="ExternalInput").ap()
    wt_in = [nc.dram_tensor(n, [D, D], F32, kind="ExternalInput").ap()
             for n in ("wqT", "wkT", "wvT", "woT")]
    wg_in = nc.dram_tensor("wgr", [128, 64], F32, kind="ExternalInput").ap()
    w1_in = nc.dram_tensor("w1", [NEXP, D, H], F32, kind="ExternalInput").ap()
    b1_in = nc.dram_tensor("b1r", [128, 128], F32, kind="ExternalInput").ap()
    w2_in = nc.dram_tensor("w2", [NEXP, H, D], F32, kind="ExternalInput").ap()
    b2_in = nc.dram_tensor("b2r", [128, 64], F32, kind="ExternalInput").ap()
    idf_in = nc.dram_tensor("identf", [128, 128], F32, kind="ExternalInput").ap()
    tok_in = nc.dram_tensor("tokids", [128, 18], F32, kind="ExternalInput").ap()
    y_out = nc.dram_tensor("y", [NBATCH * 2048, D], F32, kind="ExternalOutput").ap()

    tqs = nc.dram_tensor("tqs", [4, D, D], BF16).ap()
    w1s = nc.dram_tensor("w1s", [NEXP, 2, D, H], BF16).ap()
    w2s = nc.dram_tensor("w2s", [NEXP, 2, H, D], BF16).ap()
    x1d = nc.dram_tensor("x1d", [2176, D], F32).ap()
    gscde = [nc.dram_tensor(f"gscd{e}", [2176, 2], F32).ap() for e in range(NEXP)]
    idxe = [nc.dram_tensor(f"idxe{e}", [896, 1], F32).ap()
            for e in range(NEXP)]
    ystage = nc.dram_tensor("ystage", [8320, D], F32).ap()

    dbg = {}
    if debug:
        for n in ("dbg_a", "dbg_b", "dbg_c", "dbg_d"):
            dbg[n] = nc.dram_tensor(n, [128, 1024], F32, kind="ExternalOutput").ap()

    from contextlib import ExitStack
    es = ExitStack()

    def sb(name, shape, dt=F32):
        return es.enter_context(nc.sbuf_tensor(name, shape, dt))

    with es:
        arena = sb("arena", [128, 16384])
        wb = sb("wb", [128, 24576], BF16)
        bpool = sb("bpool", [128, 24576], BF16)
        fpool = sb("fpool", [128, 9216])
        ident_f = sb("ident_f", [128, 128])
        b1_sb = sb("b1_sb", [128, 128])
        b2_sb = sb("b2_sb", [128, 64])
        wg_h = sb("wg_h", [128, 64], BF16)
        wg_l = sb("wg_l", [128, 64], BF16)
        mw_b = sb("mw_b", [128, 4])
        wit = sb("wit", [128, 1])
        gsc = sb("gsc", [128, 32])
        smalls = sb("smalls", [128, 32])
        rowbuf = sb("rowbuf", [1, 256])
        zsrc = sb("zsrc", [128, 1024])
        ist = sb("ist", [128, 51])
        tokid_sb = sb("tokid_sb", [128, 18])
        s_idx = sb("s_idx", [128, 8], mybir.dt.int32)
        s_di = sb("s_di", [128, 8], mybir.dt.int32)
        sm16 = sb("sm16", [128, 64])

        xbuf = fpool[:, 0:1024]
        sqb = fpool[:, 1024:2048]
        qu = fpool[:, 2048:3072]
        ku = fpool[:, 3072:4096]
        vu = fpool[:, 4096:5120]
        ctxb = fpool[:, 5120:6144]
        t0 = fpool[:, 6144:7168]
        t1 = fpool[:, 7168:8192]
        scoresb = fpool[:, 8192:8448]
        pexp = fpool[:, 8448:8704]
        lzb = fpool[:, 8704:8712]
        zxb = fpool[:, 8712:8720]
        eqb = fpool[:, 8720:8728]
        qqb = fpool[:, 8728:8736]
        selb = fpool[:, 8736:8744]
        gsb = fpool[:, 8744:8752]
        selT = fpool[0:8, 0:2048]
        rankT = fpool[0:8, 2048:4096]
        slotT = fpool[0:8, 4096:6144]
        ltT = fpool[0:8, 6144:8192]
        pkall = fpool[:, 8192:8448]
        h32s = [fpool[:, 2048:2560], fpool[:, 2560:3072]]
        y_sl = [fpool[:, 3072:3584], fpool[:, 3584:4096]]
        ybuf = fpool[:, 4096:8192]
        xgs = [fpool[:, 8192:9216], fpool[:, 0:1024]]
        u0 = qu
        u1 = ku
        xres = vu

        kqTb = bpool[:, 0:1024]
        x1tT_h = bpool[:, 0:1024]
        x1tT_l = bpool[:, 1024:2048]
        x1t_h = bpool[:, 0:4096]
        x1t_l = bpool[:, 4096:8192]
        h_hi = bpool[:, 8192:16384]
        h_lo = bpool[:, 16384:24576]

        tern = [wb[:, m * 8192:(m + 1) * 8192] for m in range(3)]
        tern_o = wb[:, 0:8192]
        wbh = wb[:, 0:8192]
        wbl = wb[:, 8192:16384]

        def sl(i):
            return smalls[:, i:i + 1]
        (s_ms, s_am, s_t, s_u, s_r, s_mxn, s_mxc, s_inv, s_g, s_csc, s_cv,
         s_co, s_m1, s_mm, s_var, s_rst, s_tmp, s_tmp2, s_nm, s_gm1, s_gm2,
         s_den, s_dinv, s_wm, s_ws, s_wsb) = (sl(i) for i in range(26))

        negm16 = sm16[:, 0:16]
        zsum16 = sm16[:, 16:32]
        zinv16 = sm16[:, 32:48]
        zc16 = sm16[:, 48:64]

        PB = [es.enter_context(nc.psum_tensor(f"PB{i}", [128, 512], F32)) for i in range(8)]

        sems = {}
        for n in ("sL", "sO", "sV", "sT", "sS", "sB", "sConst", "sTern", "sX",
                  "sWS0", "sWS1", "sWS2", "sIdx", "sGx0", "sGx1", "sGn", "sSc",
                  "sSlot", "sPrep", "sY2", "sOut", "sGsc", "sG"):
            sems[n] = Ctr(es.enter_context(nc.semaphore(n)))
        (sL, sO, sV, sT, sS, sB, sConst, sTern, sX, sWS0, sWS1, sWS2, sIdx,
         sGx0, sGx1, sGn, sSc, sSlot, sPrep, sY2, sOut, sGsc, sG) = (
            sems[n] for n in ("sL", "sO", "sV", "sT", "sS", "sB", "sConst",
                              "sTern", "sX", "sWS0", "sWS1", "sWS2", "sIdx",
                              "sGx0", "sGx1", "sGn", "sSc", "sSlot", "sPrep",
                              "sY2", "sOut", "sGsc", "sG"))
        sWS = [sWS0, sWS1, sWS2]

        prog = {e: [] for e in ("sync", "vector", "tensor", "scalar", "gpsimd")}
        ENGS = ("sync", "vector", "tensor", "scalar", "gpsimd")

        def on(engine, fn):
            prog[engine].append(fn)

        def barrier():
            sB.n += 5
            tgt = sB.n
            for e in ENGS:
                def bar(eng, _h=sB.h, _t=tgt):
                    eng.nop().then_inc(_h, 1)
                    eng.wait_ge(_h, _t)
                on(e, bar)

        def dma(engine, ctr, out_ap, in_ap, **kw):
            ctr.n += 16
            tgt = ctr.n

            def go(eng, _o=out_ap, _i=in_ap, _h=ctr.h, _kw=kw):
                eng.dma_start(out=_o, in_=_i, **_kw).then_inc(_h, 16)
            on(engine, go)
            return tgt

        def idma(engine, ctr, out_ap, in_ap, out_idx=None, in_idx=None):
            ctr.n += 16
            tgt = ctr.n

            def go(eng, _o=out_ap, _i=in_ap, _oi=out_idx, _ii=in_idx, _h=ctr.h):
                eng.indirect_dma_start(
                    _o, (bass.IndirectOffsetOnAxis(_oi, 0) if _oi is not None else None),
                    _i, (bass.IndirectOffsetOnAxis(_ii, 0) if _ii is not None else None),
                ).then_inc(_h, 16)
            on(engine, go)
            return tgt

        def op(engine, fn, ctr=None, k=1):
            if ctr is None:
                on(engine, lambda eng, _f=fn: _f(eng))
                return None
            ctr.n += k
            tgt = ctr.n

            def go(eng, _f=fn, _h=ctr.h, _k=k):
                _f(eng).then_inc(_h, _k)
            on(engine, go)
            return tgt

        def wait(engine, ctr, val=None):
            v = ctr.n if val is None else val
            on(engine, lambda eng, _h=ctr.h, _v=v: eng.wait_ge(_h, _v))

        def vgate():
            t = op("vector", lambda e: e.engine_nop(), sV)
            wait("vector", sV, t)
            return t

        def rne(dst, src, tmp_a, tmp_b):
            op("vector", lambda e, a=tmp_a, s=src: e.tensor_scalar(a, s, C23, C23, ALU.add, ALU.subtract))
            op("vector", lambda e, b2=tmp_b, s=src: e.tensor_scalar(b2, s, C52, C52, ALU.add, ALU.subtract))
            op("vector", lambda e, b2=tmp_b, a=tmp_a: e.tensor_sub(b2, b2, a))
            op("vector", lambda e, d=dst, b2=tmp_b, a=tmp_a: e.scalar_tensor_tensor(
                d, b2, wit[:, :], a, ALU.mult, ALU.add))

        def ln_chain(zsrc, dst):
            t = vgate()
            wait("scalar", sV, t)
            ssq = op("scalar", lambda e, _z=zsrc: e.activation(sqb, _z, AF.Square), sS)
            wait("vector", sS, ssq)
            op("vector", lambda e, _z=zsrc: e.tensor_reduce(s_m1, _z, AX.X, ALU.add))
            v2 = op("vector", lambda e: e.tensor_reduce(s_ms, sqb, AX.X, ALU.add), sV)
            wait("vector", sV, v2)
            op("vector", lambda e: e.tensor_scalar_mul(s_m1, s_m1, 1.0 / D))
            op("vector", lambda e: e.tensor_scalar_mul(s_var, s_ms, 1.0 / D))
            vgate()
            op("vector", lambda e: e.tensor_mul(s_mm, s_m1, s_m1))
            vgate()
            op("vector", lambda e: e.tensor_sub(s_var, s_var, s_mm))
            op("vector", lambda e: e.tensor_scalar_add(s_var, s_var, LN_EPS))
            t2 = vgate()
            v3 = op("vector", lambda e: e.reciprocal(s_u, s_var), sV)
            wait("scalar", sV, v3)
            s4 = op("scalar", lambda e: e.activation(s_rst, s_u, AF.Sqrt), sS)
            wait("vector", sS, s4)
            return op("vector", lambda e, _z=zsrc, _d=dst: e.tensor_scalar(
                _d, _z, s_m1, s_rst, ALU.subtract, ALU.mult), sV)

        def quant_chain(src, sq_gate=None):
            if sq_gate is not None:
                wait("scalar", sV, sq_gate)
            ssq = op("scalar", lambda e, _s=src: e.activation(sqb, _s, AF.Square), sS)
            quant_chain.sq_done = ssq
            wait("vector", sS, ssq)
            op("vector", lambda e: e.tensor_reduce(s_ms, sqb, AX.X, ALU.add))
            vam = op("vector", lambda e, _s=src: e.tensor_reduce(
                s_am, _s, AX.X, ALU.max, apply_absolute_value=True), sV)
            wait("vector", sV, vam)
            op("vector", lambda e: e.tensor_scalar(s_t, s_ms, 1.0 / D, RMS_EPS, ALU.mult, ALU.add))
            vgate()
            vu_ = op("vector", lambda e: e.reciprocal(s_u, s_t), sV)
            wait("scalar", sV, vu_)
            sr_ = op("scalar", lambda e: e.activation(s_r, s_u, AF.Sqrt), sS)
            wait("vector", sS, sr_)
            op("vector", lambda e: e.tensor_mul(s_mxn, s_am, s_r))
            vgate()
            op("vector", lambda e: e.tensor_scalar_max(s_mxc, s_mxn, Q_EPS))
            vgate()
            vi = op("vector", lambda e: e.reciprocal(s_inv, s_mxc), sV)
            wait("vector", sV, vi)
            op("vector", lambda e: e.tensor_mul(s_g, s_r, s_inv))
            vgate()
            op("vector", lambda e: e.tensor_scalar_mul(s_g, s_g, 127.0))
            vgate()
            quant_chain.x_done = op("vector", lambda e, _s=src: e.tensor_scalar(
                t0, _s, s_g, None, ALU.mult), sV)
            rne(t1, t0, sqb, t1)
            op("vector", lambda e: e.tensor_scalar(t1, t1, 127.0, -128.0, ALU.min, ALU.max))
            return op("vector", lambda e: e.engine_nop(), sV)

        def kq_transpose():
            for g in range(2):
                for kc4 in range(4):
                    kc = g * 4 + kc4
                    op("tensor", lambda e, _k=kc, _g=g, _c=kc4: e.transpose(
                        PB[6 + _g][:, _c * 128:(_c + 1) * 128],
                        t1[:, _k * 128:(_k + 1) * 128], ident_f[:, :]), sT)
            tt = sT.n
            wait("scalar", sT, tt)
            e1 = op("scalar", lambda e: e.copy(kqTb[:, 0:512], PB[6][:, :]), sS)
            e2 = op("scalar", lambda e: e.copy(kqTb[:, 512:1024], PB[7][:, :]), sS)
            return e2

        op("vector", lambda e: e.memset(wit[:, :], 1.0))
        vgate()
        op("vector", lambda e: e.tensor_scalar(wit[:, :], wit[:, :], C52, C52, ALU.add, ALU.subtract))
        vgate()

        op("vector", lambda e: e.memset(zsrc[:, :], 0.0))
        op("vector", lambda e: e.memset(ist[:, 0:49], SENT))
        op("vector", lambda e: e.memset(ist[:, 49:51], 2.0))
        v_ini = op("vector", lambda e: e.engine_nop(), sV)
        dma("sync", sConst, ident_f[:, :], idf_in[:, :])
        dma("sync", sConst, tokid_sb[:, :], tok_in[:, :])
        wait("gpsimd", sV, v_ini)
        for e_ in range(NEXP):
            dma("gpsimd", sO, gscde[e_][2048:2176, :], ist[:, 49:51])
        dma("gpsimd", sO, x1d[2048:2176, :], zsrc[:, :])
        dma("sync", sConst, b1_sb[:, :], b1_in[:, :])
        dma("sync", sConst, b2_sb[:, :], b2_in[:, :])
        t_wg = dma("sync", sConst, sm16[:, 0:64], wg_in[:, :])
        wait("vector", sConst, t_wg)
        op("vector", lambda e: e.tensor_copy(wg_h[:, :], sm16[:, 0:64]))
        vgate()
        op("vector", lambda e: e.tensor_sub(wg_l[:, :], sm16[:, 0:64], wg_h[:, :]))
        vgate()

        wfull = arena[:, 0:8192]
        ybig = arena[:, 8192:16384]
        for m in range(4):
            t_l = dma("sync", sL, wfull.rearrange("p (k e) -> p k e", k=8),
                      wt_in[m].rearrange("(k p) e -> p k e", p=128))
            wait("vector", sL, t_l)
            if m > 0:
                wait("vector", sO, sO.n)
            v1 = op("vector", lambda e: e.tensor_reduce(
                s_tmp, wfull, AX.X, ALU.add, apply_absolute_value=True), sV)
            wait("sync", sV, v1)
            t_g1 = dma("sync", sL, rowbuf[0:1, 0:128], s_tmp)
            wait("vector", sL, t_g1)
            op("vector", lambda e: e.tensor_reduce(s_wm[0:1, :], rowbuf[0:1, 0:128],
                                                   AX.X, ALU.add))
            vgate()
            op("vector", lambda e: e.tensor_scalar(s_wm[0:1, :], s_wm[0:1, :],
                                                   1.0 / (D * D), W_EPS, ALU.mult, ALU.max))
            vgate()
            op("vector", lambda e: e.reciprocal(s_ws[0:1, :], s_wm[0:1, :]))
            vgate()
            op("vector", lambda e: e.tensor_copy(rowbuf[0:1, 0:128],
                                                 s_wm[0:1, 0:1].broadcast_to([1, 128])))
            v3 = op("vector", lambda e: e.tensor_copy(rowbuf[0:1, 128:256],
                                                      s_ws[0:1, 0:1].broadcast_to([1, 128])), sV)
            wait("sync", sV, v3)
            dma("sync", sL, mw_b[:, m:m + 1], rowbuf[0:1, 0:128])
            t_g2 = dma("sync", sL, s_wsb, rowbuf[0:1, 128:256])
            wait("vector", sL, t_g2)
            for half in range(2):
                wsl = wfull[:, half * 4096:(half + 1) * 4096]
                ysl = ybig[:, half * 4096:(half + 1) * 4096]
                bsl = bpool[:, 8192 + half * 4096:8192 + (half + 1) * 4096]
                op("vector", lambda e, w=wsl, y=ysl: e.tensor_scalar(y, w, s_wsb, None, ALU.mult))
                op("vector", lambda e, y=ysl, a=wsl: e.tensor_scalar(a, y, C23, C23, ALU.add, ALU.subtract))
                op("vector", lambda e, y=ysl: e.tensor_scalar(y, y, C52, C52, ALU.add, ALU.subtract))
                op("vector", lambda e, y=ysl, a=wsl: e.tensor_sub(y, y, a))
                op("vector", lambda e, y=ysl, a=wsl: e.scalar_tensor_tensor(
                    y, y, wit[:, :], a, ALU.mult, ALU.add))
                op("vector", lambda e, y=ysl: e.tensor_scalar(y, y, 1.0, -1.0, ALU.min, ALU.max))
                op("vector", lambda e, y=ysl, bb=bsl: e.tensor_copy(bb, y), sV)
            wait("gpsimd", sV, sV.n)
            dma("gpsimd", sO, tqs[m].rearrange("(k p) e -> p k e", p=128),
                bpool[:, 8192:16384].rearrange("p (k e) -> p k e", k=8))
            wait("sync", sO, sO.n)
        barrier()

        for eexp in range(NEXP):
            for wsrc, wdst, rows, cols in ((w1_in, w1s, D, H), (w2_in, w2s, H, D)):
                nhalf = (rows * cols) // (128 * 8192)
                rph = rows // nhalf
                for hh in range(nhalf):
                    rsl = slice(hh * rph, (hh + 1) * rph)
                    t_l = dma("sync", sL, wfull.rearrange("p (k c) -> p k c", c=cols),
                              wsrc[eexp, rsl, :].rearrange("(k p) c -> p k c", p=128))
                    wait("vector", sL, t_l)
                    wait("vector", sO, sO.n)
                    op("vector", lambda e: e.tensor_copy(bpool[:, 8192:16384], wfull))
                    vgate()
                    vl = op("vector", lambda e: e.tensor_sub(
                        bpool[:, 16384:24576], wfull, bpool[:, 8192:16384]), sV)
                    wait("gpsimd", sV, vl)
                    dma("gpsimd", sO, wdst[eexp, 0, rsl, :].rearrange("(k p) c -> p k c", p=128),
                        bpool[:, 8192:16384].rearrange("p (k c) -> p k c", c=cols))
                    dma("gpsimd", sO, wdst[eexp, 1, rsl, :].rearrange("(k p) c -> p k c", p=128),
                        bpool[:, 16384:24576].rearrange("p (k c) -> p k c", c=cols))
                    wait("sync", sO, sO.n)
        barrier()

        def cut(lvl):
            if stop is not None and stop <= lvl:
                raise _CutDone()

        try:
            for b in range(NBATCH):
                tok0 = b * 2048

                for m in range(3):
                    dma("sync", sL, tern[m].rearrange("p (k e) -> p k e", k=8),
                        tqs[m].rearrange("(k p) e -> p k e", p=128))
                tern_ld = sL.n
                wait("gpsimd", sY2, sY2.n)
                wait("gpsimd", sV, sV.n)
                for rb in range(17):
                    dma("gpsimd", sPrep, ystage[rb * 128:(rb + 1) * 128, :], zsrc[:, :])
                for rb in range(17):
                    dma("gpsimd", sPrep, ystage[4160 + rb * 128:4160 + (rb + 1) * 128, :], zsrc[:, :])
                for e_ in range(NEXP):
                    dma("gpsimd", sPrep, idxe[e_].rearrange("(p s) o -> p (s o)", p=128),
                        ist[:, 0:7])

                a1_x_done = 0
                a1_sq_done = 0
                a1_rne_done = 0
                a1_ctx_reads = 0
                for i in range(TB):
                    trow = tok0 + i * TPB
                    if i > 0 or b > 0:
                        wait("sync", sV, a1_x_done)
                        wait("sync", sS, a1_sq_done)
                    t_x = dma("sync", sL, xbuf, x_in[trow:trow + TPB, :])
                    wait("scalar", sL, t_x)
                    wait("vector", sL, t_x)
                    quant_chain(xbuf, sq_gate=a1_rne_done if (i > 0 or b > 0) else None)
                    a1_x_done = quant_chain.x_done
                    a1_sq_done = quant_chain.sq_done
                    op("vector", lambda e: e.tensor_mul(s_csc, s_mxc, s_mxc))
                    vgate()
                    op("vector", lambda e: e.tensor_scalar(s_csc, s_csc, mw_b[:, 0:1], mw_b[:, 1:2],
                                                           ALU.mult, ALU.mult))
                    op("vector", lambda e: e.tensor_scalar(s_cv, s_mxc, mw_b[:, 2:3], 1.0 / 127.0,
                                                           ALU.mult, ALU.mult))
                    v_kq = vgate()
                    a1_rne_done = v_kq
                    wait("tensor", sV, v_kq)
                    if i == 0:
                        wait("tensor", sL, tern_ld)
                    e_kqt = kq_transpose()
                    wait("tensor", sS, e_kqt)
                    for kc in range(8):
                        for p_ in range(3):
                            for eo in range(2):
                                op("tensor", lambda e, _k=kc, _p=p_, _e=eo: e.matmul(
                                    PB[_p * 2 + _e][:, :], kqTb[:, _k * 128:(_k + 1) * 128],
                                    tern[_p][:, _k * 1024 + _e * 512:_k * 1024 + (_e + 1) * 512],
                                    start=(_k == 0), stop=(_k == 7)), sT)
                    wait("scalar", sT, sT.n)
                    wait("scalar", sV, a1_ctx_reads)
                    for p_, dstb in ((0, qu), (1, ku), (2, vu)):
                        for eo in range(2):
                            op("scalar", lambda e, _p=p_, _e=eo, _d=dstb: e.copy(
                                _d[:, _e * 512:(_e + 1) * 512], PB[_p * 2 + _e][:, :]), sS)
                    ev_qkv = op("scalar", lambda e: e.nop(), sS)
                    wait("vector", sS, ev_qkv)
                    for hh in range(NH):
                        op("vector", lambda e, _h=hh: e.tensor_tensor(
                            t0.rearrange("p (t d) -> p t d", t=16),
                            qu[:, _h * 64:(_h + 1) * 64].rearrange("p (o d) -> p o d", o=1)
                              .broadcast_to([128, 16, 64]),
                            ku.rearrange("p (t d) -> p t d", t=16), ALU.mult))
                        op("vector", lambda e, _h=hh: e.tensor_reduce(
                            scoresb[:, _h * 16:(_h + 1) * 16].rearrange("p (t o) -> p t o", o=1),
                            t0.rearrange("p (t d) -> p t d", t=16), AX.X, ALU.add))
                    vgate()
                    op("vector", lambda e: e.tensor_scalar(
                        scoresb, scoresb, s_csc, 1.0 / (127.0 * 127.0 * 8.0), ALU.mult, ALU.mult))
                    vgate()
                    op("vector", lambda e: e.tensor_reduce(
                        negm16.rearrange("p (h o) -> p h o", o=1),
                        scoresb.rearrange("p (h t) -> p h t", h=16), AX.X, ALU.max, negate=True))
                    vgate()
                    v_z = op("vector", lambda e: e.tensor_tensor(
                        scoresb.rearrange("p (h t) -> p h t", h=16),
                        scoresb.rearrange("p (h t) -> p h t", h=16),
                        negm16.rearrange("p (h o) -> p h o", o=1).broadcast_to([128, 16, 16]),
                        ALU.add), sV)
                    wait("scalar", sV, v_z)
                    s_e = op("scalar", lambda e: e.activation(pexp, scoresb, AF.Exp), sS)
                    wait("vector", sS, s_e)
                    op("vector", lambda e: e.tensor_reduce(
                        zsum16.rearrange("p (h o) -> p h o", o=1),
                        pexp.rearrange("p (h t) -> p h t", h=16), AX.X, ALU.add))
                    vgate()
                    op("vector", lambda e: e.reciprocal(zinv16, zsum16))
                    vgate()
                    op("vector", lambda e: e.tensor_scalar(zc16, zinv16, s_cv, None, ALU.mult))
                    vgate()
                    for hh in range(NH):
                        op("vector", lambda e, _h=hh: e.tensor_tensor(
                            t0.rearrange("p (t d) -> p t d", t=16),
                            pexp[:, _h * 16:(_h + 1) * 16].rearrange("p (t o) -> p t o", o=1)
                                .broadcast_to([128, 16, 64]),
                            vu.rearrange("p (t d) -> p t d", t=16), ALU.mult))
                        op("vector", lambda e, _h=hh: e.tensor_reduce(
                            t1[:, _h * 64:(_h + 1) * 64].rearrange("p (d o) -> p d o", o=1),
                            t0.rearrange("p (t d) -> p d t", t=16), AX.X, ALU.add))
                    a1_ctx_reads = vgate()
                    for hh in range(NH):
                        op("vector", lambda e, _h=hh: e.tensor_scalar(
                            ctxb[:, _h * 64:(_h + 1) * 64], t1[:, _h * 64:(_h + 1) * 64],
                            zc16[:, _h:_h + 1], None, ALU.mult))
                    v_ctx = op("vector", lambda e: e.engine_nop(), sV)
                    wait("tensor", sV, v_ctx)
                    for fc in range(8):
                        bank = 6 + (fc % 2)
                        tt = op("tensor", lambda e, _f=fc, _b=bank: e.transpose(
                            PB[_b][:, 0:128], ctxb[:, _f * 128:(_f + 1) * 128], ident_f[:, :]), sT)
                        wait("scalar", sT, tt)
                        cp = op("scalar", lambda e, _f=fc, _i=i, _b=bank: e.copy(
                            arena[:, _f * 2048 + _i * 128:_f * 2048 + (_i + 1) * 128],
                            PB[_b][:, 0:128]), sS)
                        wait("tensor", sS, cp)
                barrier()

                t_to = dma("sync", sL, tern_o.rearrange("p (k e) -> p k e", k=8),
                           tqs[3].rearrange("(k p) e -> p k e", p=128))

                a2_u_done = 0
                a2_rne_done = 0
                for j in range(8):
                    slab = arena[:, j * 2048:(j + 1) * 2048]
                    wait("scalar", sV, a2_u_done)
                    for m in range(16):
                        bank = 6 + (m % 2)
                        tt = op("tensor", lambda e, _m=m, _b=bank, _s=slab: e.transpose(
                            PB[_b][:, 0:128], _s.rearrange("p (u s) -> p s u", s=16)[:, _m, :],
                            ident_f[:, :]), sT)
                        wait("scalar", sT, tt)
                        op("scalar", lambda e, _m=m, _b=bank: e.copy(
                            u0[:, _m * 64:(_m + 1) * 64], PB[_b][:, 0:64]), sS)
                        cp = op("scalar", lambda e, _m=m, _b=bank: e.copy(
                            u1[:, _m * 64:(_m + 1) * 64], PB[_b][:, 64:128]), sS)
                        wait("tensor", sS, cp)
                    if j == 0:
                        wait("tensor", sL, t_to)
                    for hsub, usrc in ((0, u0), (1, u1)):
                        hidx = 2 * j + hsub
                        trow = tok0 + hidx * TPB
                        if hsub == 1 or j > 0:
                            wait("sync", sV, sV.n)
                        t_x = dma("sync", sX, xres, x_in[trow:trow + TPB, :])
                        quant_chain(usrc, sq_gate=a2_rne_done if (hsub == 1 or j > 0) else None)
                        if hsub == 1:
                            a2_u_done = quant_chain.x_done
                        op("vector", lambda e: e.tensor_scalar(s_co, s_mxc, mw_b[:, 3:4], 1.0 / 127.0,
                                                               ALU.mult, ALU.mult))
                        v_kq = vgate()
                        a2_rne_done = v_kq
                        wait("tensor", sV, v_kq)
                        e_kqt = kq_transpose()
                        wait("tensor", sS, e_kqt)
                        for kc in range(8):
                            for eo in range(2):
                                op("tensor", lambda e, _k=kc, _e=eo: e.matmul(
                                    PB[4 + _e][:, :], kqTb[:, _k * 128:(_k + 1) * 128],
                                    tern_o[:, _k * 1024 + _e * 512:_k * 1024 + (_e + 1) * 512],
                                    start=(_k == 0), stop=(_k == 7)), sT)
                        t_mm = sT.n
                        wait("vector", sT, t_mm)
                        wait("vector", sL, t_x)
                        for eo in range(2):
                            op("vector", lambda e, _e=eo: e.scalar_tensor_tensor(
                                t0[:, _e * 512:(_e + 1) * 512], PB[4 + _e][:, :], s_co,
                                xres[:, _e * 512:(_e + 1) * 512], ALU.mult, ALU.add))
                        v_zz = op("vector", lambda e: e.engine_nop(), sV)
                        if debug and b == 0 and hidx == 0:
                            wait("gpsimd", sV, v_zz)
                            dma("gpsimd", sO, dbg["dbg_d"][:, :], t0)
                            wait("gpsimd", sO, sO.n)
                        v_x1 = ln_chain(t0, arena[:, hidx * 1024:(hidx + 1) * 1024])
                        wait("gpsimd", sV, v_x1)
                        dma("gpsimd", sPrep, x1d[hidx * TPB:(hidx + 1) * TPB, :],
                            arena[:, hidx * 1024:(hidx + 1) * 1024])
                barrier()

                if debug and b == 0:
                    dma("gpsimd", sO, dbg["dbg_c"][:, :], arena[:, 0:1024])
                    wait("gpsimd", sO, sO.n)
                cut(1)
                wait("sync", sGn, sGn.n)
                for t_ in range(TB):
                    x1t = arena[:, t_ * 1024:(t_ + 1) * 1024]
                    wait("tensor", sV, sV.n)
                    wait("tensor", sS, sS.n)
                    for g in range(2):
                        for kc4 in range(4):
                            kc = g * 4 + kc4
                            op("tensor", lambda e, _k=kc, _g=g, _c=kc4, _x=x1t: e.transpose(
                                PB[6 + _g][:, _c * 128:(_c + 1) * 128],
                                _x[:, _k * 128:(_k + 1) * 128], ident_f[:, :]), sT)
                    t_tp = sT.n
                    wait("scalar", sT, t_tp)
                    for g in range(2):
                        op("scalar", lambda e, _g=g: e.copy(
                            x1tT_h.rearrange("p (k c) -> p k c", k=8)[:, _g * 4:(_g + 1) * 4, :],
                            PB[6 + _g].rearrange("p (k c) -> p k c", k=4)), sS)
                    e_hi = sS.n
                    wait("vector", sS, e_hi)
                    for g in range(2):
                        op("vector", lambda e, _g=g: e.tensor_tensor(
                            x1tT_l.rearrange("p (k c) -> p k c", k=8)[:, _g * 4:(_g + 1) * 4, :],
                            PB[6 + _g].rearrange("p (k c) -> p k c", k=4),
                            x1tT_h.rearrange("p (k c) -> p k c", k=8)[:, _g * 4:(_g + 1) * 4, :],
                            ALU.subtract), sV)
                    v_lo = sV.n
                    wait("tensor", sV, v_lo)
                    for kc in range(8):
                        for pi, (lh, rh) in enumerate(((x1tT_h, wg_h), (x1tT_l, wg_h), (x1tT_h, wg_l))):
                            op("tensor", lambda e, _k=kc, _l=lh, _r=rh,
                               _fi=(kc == 0 and pi == 0), _la=(kc == 7 and pi == 2): e.matmul(
                                PB[3][:, 0:8],
                                _l[:, _k * 128:(_k + 1) * 128],
                                _r[:, _k * 8:(_k + 1) * 8], start=_fi, stop=_la), sT)
                    t_g = sT.n
                    wait("scalar", sT, t_g)
                    s_lz = op("scalar", lambda e: e.copy(lzb, PB[3][:, 0:8]), sS)
                    wait("vector", sS, s_lz)
                    v_nm2 = op("vector", lambda e: e.tensor_reduce(
                        s_nm, lzb, AX.X, ALU.max, negate=True), sV)
                    wait("scalar", sV, v_nm2)
                    s_zx = op("scalar", lambda e: e.activation(zxb, lzb, AF.Exp, bias=s_nm), sS)
                    wait("vector", sS, s_zx)
                    op("vector", lambda e: e.tensor_reduce(s_gm1, zxb, AX.X, ALU.max))
                    vgate()
                    op("vector", lambda e: e.tensor_scalar(eqb, zxb, s_gm1, None, ALU.is_equal))
                    vgate()
                    op("vector", lambda e: e.scalar_tensor_tensor(qqb, eqb, -4.0, zxb, ALU.mult, ALU.add))
                    vgate()
                    op("vector", lambda e: e.tensor_reduce(s_gm2, qqb, AX.X, ALU.max))
                    vgate()
                    op("vector", lambda e: e.tensor_scalar(selb, zxb, s_gm2, None, ALU.is_ge))
                    op("vector", lambda e: e.tensor_add(s_den, s_gm1, s_gm2))
                    vgate()
                    op("vector", lambda e: e.reciprocal(s_dinv, s_den))
                    op("vector", lambda e: e.tensor_mul(gsb, zxb, selb))
                    vgate()
                    op("vector", lambda e: e.tensor_tensor_scan(
                        sm16[:, 32:40], selb, selb, 0.0, ALU.add, ALU.bypass))
                    op("vector", lambda e, _t=t_: e.tensor_scalar(
                        gsc[:, 0:8], gsb, s_dinv, None, ALU.mult))
                    vgate()
                    op("vector", lambda e, _t=t_: e.tensor_copy(
                        pkall.rearrange("p (t e w) -> p t e w", t=16, w=2)[:, _t, :, 0:1],
                        gsc[:, 0:8].rearrange("p (e w) -> p e w", w=1)))
                    v_pk = op("vector", lambda e, _t=t_: e.tensor_copy(
                        pkall.rearrange("p (t e w) -> p t e w", t=16, w=2)[:, _t, :, 1:2],
                        sm16[:, 32:40].rearrange("p (e w) -> p e w", w=1)), sV)
                    wait("tensor", sV, v_pk)
                    t_sel = op("tensor", lambda e: e.transpose(
                        PB[2][0:8, 0:128], selb, ident_f[:, :]), sT)
                    wait("scalar", sT, t_sel)
                    cp_sel = op("scalar", lambda e, _t=t_: e.copy(
                        selT[:, _t * 128:(_t + 1) * 128], PB[2][0:8, 0:128]), sS)
                    wait("tensor", sS, cp_sel)

                cut(2)
                wait("vector", sS, sS.n)
                op("vector", lambda e: e.tensor_tensor_scan(
                    rankT, selT, selT, 0.0, ALU.add, ALU.bypass))
                vgate()
                op("vector", lambda e: e.tensor_sub(rankT, rankT, selT))
                vgate()
                op("vector", lambda e: e.tensor_scalar(
                    ltT, rankT, float(CAP), None, ALU.is_lt))
                vgate()
                op("vector", lambda e: e.tensor_mul(ltT, ltT, selT))
                op("vector", lambda e: e.tensor_scalar_add(slotT, rankT, -float(CAP)))
                vgate()
                op("vector", lambda e: e.tensor_mul(slotT, slotT, ltT))
                vgate()
                op("vector", lambda e: e.tensor_scalar_add(slotT, slotT, float(CAP)))
                vgate()
                v_slot = op("vector", lambda e: e.tensor_scalar(
                    slotT, slotT, 895.0, 0.0, ALU.min, ALU.max), sV)
                wait("tensor", sV, v_slot)
                prev_cp_sl = 0
                sc_done = [0, 0]
                gc_done = [0, 0]
                idx_pp = (s_idx, s_di)
                for t_ in range(TB):
                    pb_ = t_ % 2
                    if t_ > 0:
                        wait("tensor", sS, prev_cp_sl)
                    t_sl = op("tensor", lambda e, _t=t_: e.transpose(
                        PB[2][:, 0:8], slotT[:, _t * 128:(_t + 1) * 128], ident_f[0:8, 0:8]), sT)
                    wait("scalar", sT, t_sl)
                    wait("scalar", sG, gc_done[pb_])
                    cp_sl = op("scalar", lambda e, _p=pb_: e.copy(
                        sm16[:, 16 + _p * 8:24 + _p * 8], PB[2][:, 0:8]), sS)
                    prev_cp_sl = cp_sl
                    wait("gpsimd", sS, cp_sl)
                    wait("gpsimd", sSlot, sc_done[pb_])
                    gc = op("gpsimd", lambda e, _p=pb_: e.tensor_copy(
                        idx_pp[_p][:, 0:8], sm16[:, 16 + _p * 8:24 + _p * 8]), sG)
                    gc_done[pb_] = gc
                    wait("gpsimd", sG, gc)
                    if t_ == 0:
                        wait("gpsimd", sPrep, sPrep.n)
                    for e_ in range(NEXP):
                        idma("gpsimd", sSlot, idxe[e_][:, :], tokid_sb[:, t_:t_ + 1],
                             out_idx=idx_pp[pb_][:, e_:e_ + 1])
                    sc_done[pb_] = sSlot.n
                wait("sync", sV, sV.n)
                for e_ in range(NEXP):
                    dma("sync", sGsc, gscde[e_][0:2048, :].rearrange("(t p) c -> p t c", p=128),
                        pkall.rearrange("p (t e w) -> p t e w", t=16, w=2)[:, :, e_, :])
                gscde_done = sGsc.n

                cut(3)
                ygroups = ((0, 512), (512, 256))
                slot_last_T = [0, 0, 0]
                ybuf_scat_O = 0
                mac_done_V = 0
                for eexp in range(NEXP):
                    for g0, W_ in ygroups:
                        nst = W_ // 128
                        wait("sync", sSlot, sSlot.n)
                        wait("sync", sV, sV.n)
                        t_ix = dma("sync", sIdx, sm16[:, 8:8 + nst],
                            idxe[eexp][g0:g0 + W_, :].rearrange(
                            "(p s) o -> p (s o)", p=128))
                        wait("vector", sIdx, t_ix)
                        v_icl = op("vector", lambda e, _n=nst: e.tensor_scalar(
                            sm16[:, 8:8 + _n], sm16[:, 8:8 + _n], 2175.0, 0.0,
                            ALU.min, ALU.max), sV)
                        wait("gpsimd", sV, v_icl)
                        wait("gpsimd", sSlot, sSlot.n)
                        wait("gpsimd", sGx0, sGx0.n)
                        wait("gpsimd", sGx1, sGx1.n)
                        wait("gpsimd", sGn, sGn.n)
                        g_cast = op("gpsimd", lambda e, _n=nst: e.tensor_copy(
                            s_idx[:, 0:_n], sm16[:, 8:8 + _n]), sG)
                        wait("gpsimd", sG, g_cast)
                        prev_lo_V = 0
                        for st in range(nst):
                            xg = xgs[st % 2]
                            sGx = sGx0 if st % 2 == 0 else sGx1
                            if st >= 2:
                                wait("gpsimd", sT, sT.n)
                            ga = idma("gpsimd", sGx, xg, x1d[:, :],
                                      in_idx=s_idx[:, st:st + 1])
                            idma("gpsimd", sGn, sm16[:, st * 2:st * 2 + 2],
                                 gscde[eexp][:, :], in_idx=s_idx[:, st:st + 1])
                            wait("tensor", sGx, ga)
                            if st > 0:
                                wait("tensor", sS, sS.n)
                                wait("tensor", sV, prev_lo_V)
                            for g in range(2):
                                for kc4 in range(4):
                                    kc = g * 4 + kc4
                                    op("tensor", lambda e, _k=kc, _g=g, _c=kc4, _x=xg: e.transpose(
                                        PB[6 + _g][:, _c * 128:(_c + 1) * 128],
                                        _x[:, _k * 128:(_k + 1) * 128], ident_f[:, :]), sT)
                            t_tp = sT.n
                            wait("scalar", sT, t_tp)
                            for g in range(2):
                                op("scalar", lambda e, _g=g, _s=st: e.copy(
                                    x1t_h.rearrange("p (k c) -> p k c", k=8)[
                                        :, _g * 4:(_g + 1) * 4, _s * 128:(_s + 1) * 128],
                                    PB[6 + _g].rearrange("p (k c) -> p k c", k=4)), sS)
                            e_hi = sS.n
                            wait("vector", sS, e_hi)
                            for g in range(2):
                                op("vector", lambda e, _g=g, _s=st: e.tensor_tensor(
                                    x1t_l.rearrange("p (k c) -> p k c", k=8)[
                                        :, _g * 4:(_g + 1) * 4, _s * 128:(_s + 1) * 128],
                                    PB[6 + _g].rearrange("p (k c) -> p k c", k=4),
                                    x1t_h.rearrange("p (k c) -> p k c", k=8)[
                                        :, _g * 4:(_g + 1) * 4, _s * 128:(_s + 1) * 128],
                                    ALU.subtract), sV)
                            prev_lo_V = sV.n
                        v_xgT = op("vector", lambda e: e.engine_nop(), sV)
                        cut(3.2)
                        load_t = [None] * 8

                        def emit_stage_load(stg):
                            slot = wb[:, (stg % 3) * 8192:((stg % 3) + 1) * 8192]
                            s_hi = slot[:, 0:4096]
                            s_lo = slot[:, 4096:8192]
                            sws = sWS[stg % 3]
                            wait("sync", sT, slot_last_T[stg % 3])
                            if stg < 4:
                                csl = slice(stg * 512, (stg + 1) * 512)
                                dma("sync", sws, s_hi.rearrange("p (k c) -> p k c", k=8),
                                    w1s[eexp, 0, :, csl].rearrange("(k p) c -> p k c", p=128))
                                load_t[stg] = dma("sync", sws,
                                    s_lo.rearrange("p (k c) -> p k c", k=8),
                                    w1s[eexp, 1, :, csl].rearrange("(k p) c -> p k c", p=128))
                            else:
                                rsl = slice((stg - 4) * 512, (stg - 3) * 512)
                                dma("sync", sws, s_hi.rearrange("p (k c) -> p k c", k=4),
                                    w2s[eexp, 0, rsl, :].rearrange("(k p) c -> p k c", p=128))
                                load_t[stg] = dma("sync", sws,
                                    s_lo.rearrange("p (k c) -> p k c", k=4),
                                    w2s[eexp, 1, rsl, :].rearrange("(k p) c -> p k c", p=128))

                        for stg in range(3):
                            emit_stage_load(stg)
                        bank_evac_S = [0, 0]
                        h32_split_V = [0, 0]
                        for stg in range(8):
                            slot = wb[:, (stg % 3) * 8192:((stg % 3) + 1) * 8192]
                            s_hi = slot[:, 0:4096]
                            s_lo = slot[:, 4096:8192]
                            wait("tensor", sWS[stg % 3], load_t[stg])
                            if stg == 0:
                                wait("tensor", sV, max(v_xgT, mac_done_V))
                            if stg == 4:
                                wait("tensor", sS, max(bank_evac_S))
                                wait("tensor", sV, max(h32_split_V))
                            if stg < 4:
                                for hcL in range(4):
                                    hc = stg * 4 + hcL
                                    bank = hc % 2
                                    wait("tensor", sS, bank_evac_S[bank])
                                    for kc in range(8):
                                        for pi, (wsl_, xsl_) in enumerate(
                                                ((s_hi, x1t_h), (s_hi, x1t_l), (s_lo, x1t_h))):
                                            op("tensor", lambda e, _k=kc, _h=hcL, _w=wsl_,
                                               _x=xsl_, _b=bank, _W=W_,
                                               _fi=(kc == 0 and pi == 0),
                                               _la=(kc == 7 and pi == 2): e.matmul(
                                                PB[_b][:, 0:_W],
                                                _w[:, _k * 512 + _h * 128:_k * 512 + (_h + 1) * 128],
                                                _x[:, _k * 512:_k * 512 + _W],
                                                start=_fi, stop=_la), sT)
                                    t_h = sT.n
                                    wait("scalar", sT, t_h)
                                    wait("scalar", sV, h32_split_V[hc % 2])
                                    ss_h = op("scalar", lambda e, _hc=hc, _b=bank, _e=eexp,
                                              _W=W_: e.activation(
                                        h32s[_hc % 2][:, 0:_W], PB[_b][:, 0:_W], AF.Relu,
                                        bias=b1_sb[:, _e * 16 + _hc:_e * 16 + _hc + 1]), sS)
                                    bank_evac_S[bank] = ss_h
                                    wait("vector", sS, ss_h)
                                    op("vector", lambda e, _hc=hc, _W=W_: e.tensor_copy(
                                        h_hi[:, _hc * 512:_hc * 512 + _W],
                                        h32s[_hc % 2][:, 0:_W]))
                                    v_hl = op("vector", lambda e, _hc=hc, _W=W_: e.tensor_sub(
                                        h_lo[:, _hc * 512:_hc * 512 + _W],
                                        h32s[_hc % 2][:, 0:_W],
                                        h_hi[:, _hc * 512:_hc * 512 + _W]), sV)
                                    h32_split_V[hc % 2] = v_hl
                            else:
                                for dc in range(8):
                                    for kcb in range(4):
                                        hcb = (stg - 4) * 4 + kcb
                                        for pi, (wsl_, hsl_) in enumerate(
                                                ((s_hi, h_hi), (s_hi, h_lo), (s_lo, h_hi))):
                                            op("tensor", lambda e, _k=kcb, _d=dc, _w=wsl_,
                                               _x=hsl_, _hcb=hcb, _W=W_,
                                               _fi=(stg == 4 and kcb == 0 and pi == 0),
                                               _la=(stg == 7 and kcb == 3 and pi == 2):
                                               e.matmul(
                                                PB[_d][:, 0:_W],
                                                _w[:, _k * 1024 + _d * 128:_k * 1024 + (_d + 1) * 128],
                                                _x[:, _hcb * 512:_hcb * 512 + _W],
                                                start=_fi, stop=_la), sT)
                            slot_last_T[stg % 3] = sT.n
                            if stg + 3 <= 7:
                                emit_stage_load(stg + 3)
                        cut(3.5)
                        cut(3.7)
                        t_y = sT.n
                        wait("scalar", sT, t_y)
                        wait("vector", sGn, sGn.n)
                        wait("vector", sSc, ybuf_scat_O)
                        gnw = fpool[:, 1024:1024 + nst * 128]
                        for st in range(nst):
                            op("vector", lambda e, _s=st, _gw=gnw: e.tensor_scalar(
                                _gw[:, _s * 128:(_s + 1) * 128], zsrc[:, 0:128],
                                sm16[:, _s * 2:_s * 2 + 1], None, ALU.add))
                        v_gnw = op("vector", lambda e: e.engine_nop(), sV)
                        y_tp_T = [0, 0]
                        for dc in range(8):
                            wait("scalar", sT, y_tp_T[dc % 2])
                            ss_y = op("scalar", lambda e, _d=dc, _e=eexp, _W=W_: e.activation(
                                y_sl[_d % 2][:, 0:_W], PB[_d][:, 0:_W], AF.Identity,
                                bias=b2_sb[:, _e * 8 + _d:_e * 8 + _d + 1]), sS)
                            if stop is not None and stop <= 3.75:
                                continue
                            wait("tensor", sS, ss_y)
                            for st in range(nst):
                                op("tensor", lambda e, _s=st, _d=dc: e.transpose(
                                    PB[_d][:, _s * 128:(_s + 1) * 128],
                                    y_sl[_d % 2][:, _s * 128:(_s + 1) * 128],
                                    ident_f[:, :]), sT)
                            t_all = sT.n
                            y_tp_T[dc % 2] = t_all
                            if stop is not None and stop <= 3.8:
                                continue
                            wait("vector", sT, t_all)
                            if dc == 0:
                                wait("vector", sV, v_gnw)
                            op("vector", lambda e, _d=dc, _n=nst, _gw=gnw: e.tensor_tensor(
                                ybuf.rearrange("p (s q) -> p s q", q=1024)[
                                    :, 0:_n, _d * 128:(_d + 1) * 128],
                                PB[_d][:, 0:_n * 128].rearrange("p (s c) -> p s c", c=128),
                                _gw[:, 0:_n * 128].rearrange("p (s c) -> p s c", c=128),
                                ALU.mult), sV)
                        mac_done_V = sV.n
                        cut(3.79)
                        cut(3.85)
                        cut(3.9)
                        for st in range(nst):
                            op("vector", lambda e, _s=st: e.scalar_tensor_tensor(
                                sm16[:, 24 + _s:25 + _s], sm16[:, _s * 2 + 1:_s * 2 + 2],
                                YOFF, sm16[:, 8 + _s:9 + _s], ALU.mult, ALU.add))
                        vgate()
                        op("vector", lambda e, _n=nst: e.tensor_scalar_add(
                            sm16[:, 24:24 + _n], sm16[:, 24:24 + _n], -YOFF))
                        vgate()
                        v_di = op("vector", lambda e, _n=nst: e.tensor_scalar(
                            sm16[:, 24:24 + _n], sm16[:, 24:24 + _n], 8319.0, 0.0,
                            ALU.min, ALU.max), sV)
                        wait("gpsimd", sV, max(v_di, mac_done_V))
                        wait("gpsimd", sSc, sSc.n)
                        g_cd = op("gpsimd", lambda e, _n=nst: e.tensor_copy(
                            s_di[:, 0:_n], sm16[:, 24:24 + _n]), sG)
                        wait("gpsimd", sG, g_cd)
                        wait("gpsimd", sPrep, sPrep.n)
                        for st in range(nst):
                            idma("gpsimd", sSc, ystage[:, :],
                                 ybuf[:, st * 1024:(st + 1) * 1024],
                                 out_idx=s_di[:, st:st + 1])
                        ybuf_scat_O = sSc.n
                        cut(4 + eexp * 2 + (1 if g0 else 0))

                wait("sync", sSc, sSc.n)
                for t_ in range(TB):
                    if t_ > 0:
                        wait("sync", sV, sV.n)
                    l1 = dma("sync", sY2, ybuf[:, 0:1024], ystage[t_ * 128:(t_ + 1) * 128, :])
                    l2 = dma("sync", sY2, ybuf[:, 1024:2048],
                             ystage[4160 + t_ * 128:4160 + (t_ + 1) * 128, :])
                    wait("vector", sY2, l2)
                    wait("vector", sOut, sOut.n)
                    op("vector", lambda e: e.tensor_add(t0, ybuf[:, 0:1024], ybuf[:, 1024:2048]))
                    vgate()
                    op("vector", lambda e, _t=t_: e.tensor_add(
                        t0, t0, arena[:, _t * 1024:(_t + 1) * 1024]))
                    v_o = ln_chain(t0, t1)
                    wait("gpsimd", sV, v_o)
                    trow = tok0 + t_ * TPB
                    dma("gpsimd", sOut, y_out[trow:trow + TPB, :], t1)
                barrier()


        except _CutDone:
            pass
        wait("gpsimd", sO, sO.n)
        wait("gpsimd", sOut, sOut.n)
        wait("gpsimd", sSc, sSc.n)
        wait("gpsimd", sPrep, sPrep.n)
        wait("gpsimd", sSlot, sSlot.n)

        block = es.enter_context(nc.Block())

        @block.sync
        def _(eng):
            for fn in prog["sync"]:
                fn(eng)

        @block.vector
        def _(eng):
            for fn in prog["vector"]:
                fn(eng)

        @block.tensor
        def _(eng):
            for fn in prog["tensor"]:
                fn(eng)

        @block.scalar
        def _(eng):
            for fn in prog["scalar"]:
                fn(eng)

        @block.gpsimd
        def _(eng):
            for fn in prog["gpsimd"]:
                fn(eng)

    return nc


def kernel(L0_attn_x, L0_attn_wq, L0_attn_wk, L0_attn_wv, L0_attn_wo,
           L0_ffn_Wg, L0_ffn_W1, L0_ffn_b1, L0_ffn_W2, L0_ffn_b2, _debug=False):
    x = np.ascontiguousarray(np.asarray(L0_attn_x, dtype=np.float32))
    assert x.shape == (16, 2048, 1024)

    wqT = np.ascontiguousarray(np.asarray(L0_attn_wq, np.float32).T)
    wkT = np.ascontiguousarray(np.asarray(L0_attn_wk, np.float32).T)
    wvT = np.ascontiguousarray(np.asarray(L0_attn_wv, np.float32).T)
    woT = np.ascontiguousarray(np.asarray(L0_attn_wo, np.float32).T)
    w1 = np.ascontiguousarray(np.asarray(L0_ffn_W1, np.float32))
    w2 = np.ascontiguousarray(np.asarray(L0_ffn_W2, np.float32))
    b1r = np.ascontiguousarray(
        np.asarray(L0_ffn_b1, np.float32).reshape(8, 16, 128).transpose(2, 0, 1).reshape(128, 128))
    b2r = np.ascontiguousarray(
        np.asarray(L0_ffn_b2, np.float32).reshape(8, 8, 128).transpose(2, 0, 1).reshape(128, 64))
    wgr = np.ascontiguousarray(
        np.asarray(L0_ffn_Wg, np.float32).reshape(8, 128, 8).transpose(1, 0, 2).reshape(128, 64))
    identf = np.eye(128, dtype=np.float32)
    tokids = np.zeros((128, 18), dtype=np.float32)
    for t in range(16):
        tokids[:, t] = t * 128 + np.arange(128)
    tokids[:, 16] = (np.arange(128) % 8) * CAP

    nc = build_program(debug=_debug)
    in_maps = []
    for c in range(NCORE):
        xc = np.ascontiguousarray(x[2 * c:2 * c + 2].reshape(4096, 1024))
        in_maps.append({
            "x": xc, "wqT": wqT, "wkT": wkT, "wvT": wvT, "woT": woT,
            "wgr": wgr, "w1": w1, "b1r": b1r, "w2": w2, "b2r": b2r,
            "identf": identf, "tokids": tokids,
        })
    res = run_bass_kernel_spmd(nc, in_maps, core_ids=list(range(NCORE)))
    out = np.empty((16, 2048, 1024), dtype=np.float32)
    for c in range(NCORE):
        out[2 * c:2 * c + 2] = np.asarray(res.results[c]["y"]).reshape(2, 2048, 1024)
    if _debug:
        return out, res
    return out


# version 11 (final)
# speedup vs baseline: 1.0184x; 1.0184x over previous
import numpy as np
import ml_dtypes
import concourse.bass as bass
import concourse.mybir as mybir
from concourse.bass_utils import run_bass_kernel_spmd

F32 = mybir.dt.float32
BF16 = mybir.dt.bfloat16
AF = mybir.ActivationFunctionType
ALU = mybir.AluOpType
AX = mybir.AxisListType

C23 = 12582912.0
C52 = 6755399441055744.0
D = 1024
H = 2048
NEXP = 8
NH = 16
TPB = 128
TB = 16
NBATCH = 2
NCORE = 8

CAP = 768
DUMP = 6144.0
YOFF = 4160.0
SENT = 2048.0

RMS_EPS = 1e-6
Q_EPS = 1e-5
W_EPS = 1e-5
LN_EPS = 1e-5


class Ctr:
    def __init__(self, h):
        self.h = h
        self.n = 0


class _CutDone(Exception):
    pass


def build_program(debug=False, stop=None):
    nc = bass.Bass(target_bir_lowering=False, trn_type="TRN2", detect_race_conditions=False)

    x_in = nc.dram_tensor("x", [NBATCH * 2048, D], F32, kind="ExternalInput").ap()
    wt_in = [nc.dram_tensor(n, [D, D], F32, kind="ExternalInput").ap()
             for n in ("wqT", "wkT", "wvT", "woT")]
    wg_in = nc.dram_tensor("wgr", [128, 64], F32, kind="ExternalInput").ap()
    w1_in = nc.dram_tensor("w1", [NEXP, D, H], F32, kind="ExternalInput").ap()
    b1_in = nc.dram_tensor("b1r", [128, 128], F32, kind="ExternalInput").ap()
    w2_in = nc.dram_tensor("w2", [NEXP, H, D], F32, kind="ExternalInput").ap()
    b2_in = nc.dram_tensor("b2r", [128, 64], F32, kind="ExternalInput").ap()
    idf_in = nc.dram_tensor("identf", [128, 128], F32, kind="ExternalInput").ap()
    tok_in = nc.dram_tensor("tokids", [128, 18], F32, kind="ExternalInput").ap()
    y_out = nc.dram_tensor("y", [NBATCH * 2048, D], F32, kind="ExternalOutput").ap()

    tqs = nc.dram_tensor("tqs", [4, D, D], BF16).ap()
    w1s = nc.dram_tensor("w1s", [NEXP, 2, D, H], BF16).ap()
    w2s = nc.dram_tensor("w2s", [NEXP, 2, H, D], BF16).ap()
    x1d = nc.dram_tensor("x1d", [2176, D], F32).ap()
    gscde = [nc.dram_tensor(f"gscd{e}", [2176, 2], F32).ap() for e in range(NEXP)]
    idxe = [nc.dram_tensor(f"idxe{e}", [896, 1], F32).ap()
            for e in range(NEXP)]
    ystage = nc.dram_tensor("ystage", [8320, D], F32).ap()

    dbg = {}
    if debug:
        for n in ("dbg_a", "dbg_b", "dbg_c", "dbg_d"):
            dbg[n] = nc.dram_tensor(n, [128, 1024], F32, kind="ExternalOutput").ap()

    from contextlib import ExitStack
    es = ExitStack()

    def sb(name, shape, dt=F32):
        return es.enter_context(nc.sbuf_tensor(name, shape, dt))

    with es:
        arena = sb("arena", [128, 16384])
        wb = sb("wb", [128, 24576], BF16)
        bpool = sb("bpool", [128, 24576], BF16)
        fpool = sb("fpool", [128, 9216])
        ident_f = sb("ident_f", [128, 128])
        b1_sb = sb("b1_sb", [128, 128])
        b2_sb = sb("b2_sb", [128, 64])
        wg_h = sb("wg_h", [128, 64], BF16)
        wg_l = sb("wg_l", [128, 64], BF16)
        mw_b = sb("mw_b", [128, 4])
        wit = sb("wit", [128, 1])
        gsc = sb("gsc", [128, 32])
        smalls = sb("smalls", [128, 32])
        rowbuf = sb("rowbuf", [1, 256])
        zsrc = sb("zsrc", [128, 1024])
        ist = sb("ist", [128, 51])
        tokid_sb = sb("tokid_sb", [128, 18])
        s_idx = sb("s_idx", [128, 8], mybir.dt.int32)
        s_di = sb("s_di", [128, 8], mybir.dt.int32)
        sm16 = sb("sm16", [128, 64])

        xbuf = fpool[:, 0:1024]
        sqb = fpool[:, 1024:2048]
        qu = fpool[:, 2048:3072]
        ku = fpool[:, 3072:4096]
        vu = fpool[:, 4096:5120]
        ctxb = fpool[:, 5120:6144]
        t0 = fpool[:, 6144:7168]
        t1 = fpool[:, 7168:8192]
        scoresb = fpool[:, 8192:8448]
        pexp = fpool[:, 8448:8704]
        lzb = fpool[:, 8704:8712]
        zxb = fpool[:, 8712:8720]
        eqb = fpool[:, 8720:8728]
        qqb = fpool[:, 8728:8736]
        selb = fpool[:, 8736:8744]
        gsb = fpool[:, 8744:8752]
        selT = fpool[0:8, 0:2048]
        rankT = fpool[0:8, 2048:4096]
        slotT = fpool[0:8, 4096:6144]
        ltT = fpool[0:8, 6144:8192]
        pkall = fpool[:, 8192:8448]
        h32s = [fpool[:, 2048:2560], fpool[:, 2560:3072]]
        y_sl = [fpool[:, 3072:3584], fpool[:, 3584:4096]]
        ybuf = fpool[:, 4096:8192]
        xgs = [fpool[:, 8192:9216], fpool[:, 0:1024]]
        u0 = qu
        u1 = ku
        xres = vu

        kqTb = bpool[:, 0:1024]
        x1tT_h = bpool[:, 0:1024]
        x1tT_l = bpool[:, 1024:2048]
        x1t_h = bpool[:, 0:4096]
        x1t_l = bpool[:, 4096:8192]
        h_hi = bpool[:, 8192:16384]
        h_lo = bpool[:, 16384:24576]

        tern = [wb[:, m * 8192:(m + 1) * 8192] for m in range(3)]
        tern_o = wb[:, 0:8192]
        wbh = wb[:, 0:8192]
        wbl = wb[:, 8192:16384]

        def sl(i):
            return smalls[:, i:i + 1]
        (s_ms, s_am, s_t, s_u, s_r, s_mxn, s_mxc, s_inv, s_g, s_csc, s_cv,
         s_co, s_m1, s_mm, s_var, s_rst, s_tmp, s_tmp2, s_nm, s_gm1, s_gm2,
         s_den, s_dinv, s_wm, s_ws, s_wsb) = (sl(i) for i in range(26))

        negm16 = sm16[:, 0:16]
        zsum16 = sm16[:, 16:32]
        zinv16 = sm16[:, 32:48]
        zc16 = sm16[:, 48:64]

        PB = [es.enter_context(nc.psum_tensor(f"PB{i}", [128, 512], F32)) for i in range(8)]

        sems = {}
        for n in ("sL", "sO", "sV", "sT", "sS", "sB", "sConst", "sTern", "sX",
                  "sWS0", "sWS1", "sWS2", "sIdx", "sGx0", "sGx1", "sGn", "sSc",
                  "sSlot", "sPrep", "sY2", "sOut", "sGsc", "sG"):
            sems[n] = Ctr(es.enter_context(nc.semaphore(n)))
        (sL, sO, sV, sT, sS, sB, sConst, sTern, sX, sWS0, sWS1, sWS2, sIdx,
         sGx0, sGx1, sGn, sSc, sSlot, sPrep, sY2, sOut, sGsc, sG) = (
            sems[n] for n in ("sL", "sO", "sV", "sT", "sS", "sB", "sConst",
                              "sTern", "sX", "sWS0", "sWS1", "sWS2", "sIdx",
                              "sGx0", "sGx1", "sGn", "sSc", "sSlot", "sPrep",
                              "sY2", "sOut", "sGsc", "sG"))
        sWS = [sWS0, sWS1, sWS2]

        prog = {e: [] for e in ("sync", "vector", "tensor", "scalar", "gpsimd")}
        ENGS = ("sync", "vector", "tensor", "scalar", "gpsimd")

        def on(engine, fn):
            prog[engine].append(fn)

        def barrier():
            sB.n += 5
            tgt = sB.n
            for e in ENGS:
                def bar(eng, _h=sB.h, _t=tgt):
                    eng.nop().then_inc(_h, 1)
                    eng.wait_ge(_h, _t)
                on(e, bar)

        def dma(engine, ctr, out_ap, in_ap, **kw):
            ctr.n += 16
            tgt = ctr.n

            def go(eng, _o=out_ap, _i=in_ap, _h=ctr.h, _kw=kw):
                eng.dma_start(out=_o, in_=_i, **_kw).then_inc(_h, 16)
            on(engine, go)
            return tgt

        def idma(engine, ctr, out_ap, in_ap, out_idx=None, in_idx=None):
            ctr.n += 16
            tgt = ctr.n

            def go(eng, _o=out_ap, _i=in_ap, _oi=out_idx, _ii=in_idx, _h=ctr.h):
                eng.indirect_dma_start(
                    _o, (bass.IndirectOffsetOnAxis(_oi, 0) if _oi is not None else None),
                    _i, (bass.IndirectOffsetOnAxis(_ii, 0) if _ii is not None else None),
                ).then_inc(_h, 16)
            on(engine, go)
            return tgt

        def op(engine, fn, ctr=None, k=1):
            if ctr is None:
                on(engine, lambda eng, _f=fn: _f(eng))
                return None
            ctr.n += k
            tgt = ctr.n

            def go(eng, _f=fn, _h=ctr.h, _k=k):
                _f(eng).then_inc(_h, _k)
            on(engine, go)
            return tgt

        def wait(engine, ctr, val=None):
            v = ctr.n if val is None else val
            on(engine, lambda eng, _h=ctr.h, _v=v: eng.wait_ge(_h, _v))

        def vgate():
            t = op("vector", lambda e: e.engine_nop(), sV)
            wait("vector", sV, t)
            return t

        def rne(dst, src, tmp_a, tmp_b):
            op("vector", lambda e, a=tmp_a, s=src: e.tensor_scalar(a, s, C23, C23, ALU.add, ALU.subtract))
            op("vector", lambda e, b2=tmp_b, s=src: e.tensor_scalar(b2, s, C52, C52, ALU.add, ALU.subtract))
            op("vector", lambda e, b2=tmp_b, a=tmp_a: e.tensor_sub(b2, b2, a))
            op("vector", lambda e, d=dst, b2=tmp_b, a=tmp_a: e.scalar_tensor_tensor(
                d, b2, wit[:, :], a, ALU.mult, ALU.add))

        def ln_chain(zsrc, dst):
            t = vgate()
            wait("scalar", sV, t)
            ssq = op("scalar", lambda e, _z=zsrc: e.activation(sqb, _z, AF.Square), sS)
            wait("vector", sS, ssq)
            op("vector", lambda e, _z=zsrc: e.tensor_reduce(s_m1, _z, AX.X, ALU.add))
            v2 = op("vector", lambda e: e.tensor_reduce(s_ms, sqb, AX.X, ALU.add), sV)
            wait("vector", sV, v2)
            op("vector", lambda e: e.tensor_scalar_mul(s_m1, s_m1, 1.0 / D))
            op("vector", lambda e: e.tensor_scalar_mul(s_var, s_ms, 1.0 / D))
            vgate()
            op("vector", lambda e: e.tensor_mul(s_mm, s_m1, s_m1))
            vgate()
            op("vector", lambda e: e.tensor_sub(s_var, s_var, s_mm))
            op("vector", lambda e: e.tensor_scalar_add(s_var, s_var, LN_EPS))
            t2 = vgate()
            v3 = op("vector", lambda e: e.reciprocal(s_u, s_var), sV)
            wait("scalar", sV, v3)
            s4 = op("scalar", lambda e: e.activation(s_rst, s_u, AF.Sqrt), sS)
            wait("vector", sS, s4)
            return op("vector", lambda e, _z=zsrc, _d=dst: e.tensor_scalar(
                _d, _z, s_m1, s_rst, ALU.subtract, ALU.mult), sV)

        def quant_chain(src, sq_gate=None):
            if sq_gate is not None:
                wait("scalar", sV, sq_gate)
            ssq = op("scalar", lambda e, _s=src: e.activation(sqb, _s, AF.Square), sS)
            quant_chain.sq_done = ssq
            wait("vector", sS, ssq)
            op("vector", lambda e: e.tensor_reduce(s_ms, sqb, AX.X, ALU.add))
            vam = op("vector", lambda e, _s=src: e.tensor_reduce(
                s_am, _s, AX.X, ALU.max, apply_absolute_value=True), sV)
            wait("vector", sV, vam)
            op("vector", lambda e: e.tensor_scalar(s_t, s_ms, 1.0 / D, RMS_EPS, ALU.mult, ALU.add))
            vgate()
            vu_ = op("vector", lambda e: e.reciprocal(s_u, s_t), sV)
            wait("scalar", sV, vu_)
            sr_ = op("scalar", lambda e: e.activation(s_r, s_u, AF.Sqrt), sS)
            wait("vector", sS, sr_)
            op("vector", lambda e: e.tensor_mul(s_mxn, s_am, s_r))
            vgate()
            op("vector", lambda e: e.tensor_scalar_max(s_mxc, s_mxn, Q_EPS))
            vgate()
            vi = op("vector", lambda e: e.reciprocal(s_inv, s_mxc), sV)
            wait("vector", sV, vi)
            op("vector", lambda e: e.tensor_mul(s_g, s_r, s_inv))
            vgate()
            op("vector", lambda e: e.tensor_scalar_mul(s_g, s_g, 127.0))
            vgate()
            quant_chain.x_done = op("vector", lambda e, _s=src: e.tensor_scalar(
                t0, _s, s_g, None, ALU.mult), sV)
            rne(t1, t0, sqb, t1)
            op("vector", lambda e: e.tensor_scalar(t1, t1, 127.0, -128.0, ALU.min, ALU.max))
            return op("vector", lambda e: e.engine_nop(), sV)

        def kq_transpose():
            for g in range(2):
                for kc4 in range(4):
                    kc = g * 4 + kc4
                    op("tensor", lambda e, _k=kc, _g=g, _c=kc4: e.transpose(
                        PB[6 + _g][:, _c * 128:(_c + 1) * 128],
                        t1[:, _k * 128:(_k + 1) * 128], ident_f[:, :]), sT)
            tt = sT.n
            wait("scalar", sT, tt)
            e1 = op("scalar", lambda e: e.copy(kqTb[:, 0:512], PB[6][:, :]), sS)
            e2 = op("scalar", lambda e: e.copy(kqTb[:, 512:1024], PB[7][:, :]), sS)
            return e2

        op("vector", lambda e: e.memset(wit[:, :], 1.0))
        vgate()
        op("vector", lambda e: e.tensor_scalar(wit[:, :], wit[:, :], C52, C52, ALU.add, ALU.subtract))
        vgate()

        op("vector", lambda e: e.memset(zsrc[:, :], 0.0))
        op("vector", lambda e: e.memset(ist[:, 0:49], SENT))
        op("vector", lambda e: e.memset(ist[:, 49:51], 2.0))
        v_ini = op("vector", lambda e: e.engine_nop(), sV)
        dma("sync", sConst, ident_f[:, :], idf_in[:, :])
        dma("sync", sConst, tokid_sb[:, :], tok_in[:, :])
        wait("gpsimd", sV, v_ini)
        for e_ in range(NEXP):
            dma("gpsimd", sO, gscde[e_][2048:2176, :], ist[:, 49:51])
        dma("gpsimd", sO, x1d[2048:2176, :], zsrc[:, :])
        dma("sync", sConst, b1_sb[:, :], b1_in[:, :])
        dma("sync", sConst, b2_sb[:, :], b2_in[:, :])
        t_wg = dma("sync", sConst, sm16[:, 0:64], wg_in[:, :])
        wait("vector", sConst, t_wg)
        op("vector", lambda e: e.tensor_copy(wg_h[:, :], sm16[:, 0:64]))
        vgate()
        op("vector", lambda e: e.tensor_sub(wg_l[:, :], sm16[:, 0:64], wg_h[:, :]))
        vgate()

        wfull = arena[:, 0:8192]
        ybig = arena[:, 8192:16384]
        for m in range(4):
            t_l = dma("sync", sL, wfull.rearrange("p (k e) -> p k e", k=8),
                      wt_in[m].rearrange("(k p) e -> p k e", p=128))
            wait("vector", sL, t_l)
            if m > 0:
                wait("vector", sO, sO.n)
            v1 = op("vector", lambda e: e.tensor_reduce(
                s_tmp, wfull, AX.X, ALU.add, apply_absolute_value=True), sV)
            wait("sync", sV, v1)
            t_g1 = dma("sync", sL, rowbuf[0:1, 0:128], s_tmp)
            wait("vector", sL, t_g1)
            op("vector", lambda e: e.tensor_reduce(s_wm[0:1, :], rowbuf[0:1, 0:128],
                                                   AX.X, ALU.add))
            vgate()
            op("vector", lambda e: e.tensor_scalar(s_wm[0:1, :], s_wm[0:1, :],
                                                   1.0 / (D * D), W_EPS, ALU.mult, ALU.max))
            vgate()
            op("vector", lambda e: e.reciprocal(s_ws[0:1, :], s_wm[0:1, :]))
            vgate()
            op("vector", lambda e: e.tensor_copy(rowbuf[0:1, 0:128],
                                                 s_wm[0:1, 0:1].broadcast_to([1, 128])))
            v3 = op("vector", lambda e: e.tensor_copy(rowbuf[0:1, 128:256],
                                                      s_ws[0:1, 0:1].broadcast_to([1, 128])), sV)
            wait("sync", sV, v3)
            dma("sync", sL, mw_b[:, m:m + 1], rowbuf[0:1, 0:128])
            t_g2 = dma("sync", sL, s_wsb, rowbuf[0:1, 128:256])
            wait("vector", sL, t_g2)
            for half in range(2):
                wsl = wfull[:, half * 4096:(half + 1) * 4096]
                ysl = ybig[:, half * 4096:(half + 1) * 4096]
                bsl = bpool[:, 8192 + half * 4096:8192 + (half + 1) * 4096]
                op("vector", lambda e, w=wsl, y=ysl: e.tensor_scalar(y, w, s_wsb, None, ALU.mult))
                op("vector", lambda e, y=ysl, a=wsl: e.tensor_scalar(a, y, C23, C23, ALU.add, ALU.subtract))
                op("vector", lambda e, y=ysl: e.tensor_scalar(y, y, C52, C52, ALU.add, ALU.subtract))
                op("vector", lambda e, y=ysl, a=wsl: e.tensor_sub(y, y, a))
                op("vector", lambda e, y=ysl, a=wsl: e.scalar_tensor_tensor(
                    y, y, wit[:, :], a, ALU.mult, ALU.add))
                op("vector", lambda e, y=ysl: e.tensor_scalar(y, y, 1.0, -1.0, ALU.min, ALU.max))
                op("vector", lambda e, y=ysl, bb=bsl: e.tensor_copy(bb, y), sV)
            wait("gpsimd", sV, sV.n)
            dma("gpsimd", sO, tqs[m].rearrange("(k p) e -> p k e", p=128),
                bpool[:, 8192:16384].rearrange("p (k e) -> p k e", k=8))
            wait("sync", sO, sO.n)
        barrier()

        for eexp in range(NEXP):
            for wsrc, wdst, rows, cols in ((w1_in, w1s, D, H), (w2_in, w2s, H, D)):
                nhalf = (rows * cols) // (128 * 8192)
                rph = rows // nhalf
                for hh in range(nhalf):
                    rsl = slice(hh * rph, (hh + 1) * rph)
                    t_l = dma("sync", sL, wfull.rearrange("p (k c) -> p k c", c=cols),
                              wsrc[eexp, rsl, :].rearrange("(k p) c -> p k c", p=128))
                    wait("vector", sL, t_l)
                    wait("vector", sO, sO.n)
                    op("vector", lambda e: e.tensor_copy(bpool[:, 8192:16384], wfull))
                    vgate()
                    vl = op("vector", lambda e: e.tensor_sub(
                        bpool[:, 16384:24576], wfull, bpool[:, 8192:16384]), sV)
                    wait("gpsimd", sV, vl)
                    dma("gpsimd", sO, wdst[eexp, 0, rsl, :].rearrange("(k p) c -> p k c", p=128),
                        bpool[:, 8192:16384].rearrange("p (k c) -> p k c", c=cols))
                    dma("gpsimd", sO, wdst[eexp, 1, rsl, :].rearrange("(k p) c -> p k c", p=128),
                        bpool[:, 16384:24576].rearrange("p (k c) -> p k c", c=cols))
                    wait("sync", sO, sO.n)
        barrier()

        def cut(lvl):
            if stop is not None and stop <= lvl:
                raise _CutDone()

        try:
            for b in range(NBATCH):
                tok0 = b * 2048

                for m in range(3):
                    dma("sync", sL, tern[m].rearrange("p (k e) -> p k e", k=8),
                        tqs[m].rearrange("(k p) e -> p k e", p=128))
                tern_ld = sL.n
                wait("gpsimd", sY2, sY2.n)
                wait("gpsimd", sV, sV.n)
                for rb in range(17):
                    dma("gpsimd", sPrep, ystage[rb * 128:(rb + 1) * 128, :], zsrc[:, :])
                for rb in range(17):
                    dma("gpsimd", sPrep, ystage[4160 + rb * 128:4160 + (rb + 1) * 128, :], zsrc[:, :])
                for e_ in range(NEXP):
                    dma("gpsimd", sPrep, idxe[e_].rearrange("(p s) o -> p (s o)", p=128),
                        ist[:, 0:7])

                a1_x_done = 0
                a1_sq_done = 0
                a1_rne_done = 0
                a1_ctx_reads = 0
                for i in range(TB):
                    trow = tok0 + i * TPB
                    if i > 0 or b > 0:
                        wait("sync", sV, a1_x_done)
                        wait("sync", sS, a1_sq_done)
                    t_x = dma("sync", sL, xbuf, x_in[trow:trow + TPB, :])
                    wait("scalar", sL, t_x)
                    wait("vector", sL, t_x)
                    quant_chain(xbuf, sq_gate=a1_rne_done if (i > 0 or b > 0) else None)
                    a1_x_done = quant_chain.x_done
                    a1_sq_done = quant_chain.sq_done
                    op("vector", lambda e: e.tensor_mul(s_csc, s_mxc, s_mxc))
                    vgate()
                    op("vector", lambda e: e.tensor_scalar(s_csc, s_csc, mw_b[:, 0:1], mw_b[:, 1:2],
                                                           ALU.mult, ALU.mult))
                    op("vector", lambda e: e.tensor_scalar(s_cv, s_mxc, mw_b[:, 2:3], 1.0 / 127.0,
                                                           ALU.mult, ALU.mult))
                    v_kq = vgate()
                    a1_rne_done = v_kq
                    wait("tensor", sV, v_kq)
                    if i == 0:
                        wait("tensor", sL, tern_ld)
                    e_kqt = kq_transpose()
                    wait("tensor", sS, e_kqt)
                    for kc in range(8):
                        for p_ in range(3):
                            for eo in range(2):
                                op("tensor", lambda e, _k=kc, _p=p_, _e=eo: e.matmul(
                                    PB[_p * 2 + _e][:, :], kqTb[:, _k * 128:(_k + 1) * 128],
                                    tern[_p][:, _k * 1024 + _e * 512:_k * 1024 + (_e + 1) * 512],
                                    start=(_k == 0), stop=(_k == 7)), sT)
                    wait("scalar", sT, sT.n)
                    wait("scalar", sV, a1_ctx_reads)
                    for p_, dstb in ((0, qu), (1, ku), (2, vu)):
                        for eo in range(2):
                            op("scalar", lambda e, _p=p_, _e=eo, _d=dstb: e.copy(
                                _d[:, _e * 512:(_e + 1) * 512], PB[_p * 2 + _e][:, :]), sS)
                    ev_qkv = op("scalar", lambda e: e.nop(), sS)
                    wait("vector", sS, ev_qkv)
                    for hh in range(NH):
                        op("vector", lambda e, _h=hh: e.tensor_tensor(
                            t0.rearrange("p (t d) -> p t d", t=16),
                            qu[:, _h * 64:(_h + 1) * 64].rearrange("p (o d) -> p o d", o=1)
                              .broadcast_to([128, 16, 64]),
                            ku.rearrange("p (t d) -> p t d", t=16), ALU.mult))
                        op("vector", lambda e, _h=hh: e.tensor_reduce(
                            scoresb[:, _h * 16:(_h + 1) * 16].rearrange("p (t o) -> p t o", o=1),
                            t0.rearrange("p (t d) -> p t d", t=16), AX.X, ALU.add))
                    vgate()
                    op("vector", lambda e: e.tensor_scalar(
                        scoresb, scoresb, s_csc, 1.0 / (127.0 * 127.0 * 8.0), ALU.mult, ALU.mult))
                    vgate()
                    op("vector", lambda e: e.tensor_reduce(
                        negm16.rearrange("p (h o) -> p h o", o=1),
                        scoresb.rearrange("p (h t) -> p h t", h=16), AX.X, ALU.max, negate=True))
                    vgate()
                    v_z = op("vector", lambda e: e.tensor_tensor(
                        scoresb.rearrange("p (h t) -> p h t", h=16),
                        scoresb.rearrange("p (h t) -> p h t", h=16),
                        negm16.rearrange("p (h o) -> p h o", o=1).broadcast_to([128, 16, 16]),
                        ALU.add), sV)
                    wait("scalar", sV, v_z)
                    s_e = op("scalar", lambda e: e.activation(pexp, scoresb, AF.Exp), sS)
                    wait("vector", sS, s_e)
                    op("vector", lambda e: e.tensor_reduce(
                        zsum16.rearrange("p (h o) -> p h o", o=1),
                        pexp.rearrange("p (h t) -> p h t", h=16), AX.X, ALU.add))
                    vgate()
                    op("vector", lambda e: e.reciprocal(zinv16, zsum16))
                    vgate()
                    op("vector", lambda e: e.tensor_scalar(zc16, zinv16, s_cv, None, ALU.mult))
                    vgate()
                    for hh in range(NH):
                        op("vector", lambda e, _h=hh: e.tensor_tensor(
                            t0.rearrange("p (t d) -> p t d", t=16),
                            pexp[:, _h * 16:(_h + 1) * 16].rearrange("p (t o) -> p t o", o=1)
                                .broadcast_to([128, 16, 64]),
                            vu.rearrange("p (t d) -> p t d", t=16), ALU.mult))
                        op("vector", lambda e, _h=hh: e.tensor_reduce(
                            t1[:, _h * 64:(_h + 1) * 64].rearrange("p (d o) -> p d o", o=1),
                            t0.rearrange("p (t d) -> p d t", t=16), AX.X, ALU.add))
                    a1_ctx_reads = vgate()
                    for hh in range(NH):
                        op("vector", lambda e, _h=hh: e.tensor_scalar(
                            ctxb[:, _h * 64:(_h + 1) * 64], t1[:, _h * 64:(_h + 1) * 64],
                            zc16[:, _h:_h + 1], None, ALU.mult))
                    v_ctx = op("vector", lambda e: e.engine_nop(), sV)
                    wait("tensor", sV, v_ctx)
                    for fc in range(8):
                        bank = 6 + (fc % 2)
                        tt = op("tensor", lambda e, _f=fc, _b=bank: e.transpose(
                            PB[_b][:, 0:128], ctxb[:, _f * 128:(_f + 1) * 128], ident_f[:, :]), sT)
                        wait("scalar", sT, tt)
                        cp = op("scalar", lambda e, _f=fc, _i=i, _b=bank: e.copy(
                            arena[:, _f * 2048 + _i * 128:_f * 2048 + (_i + 1) * 128],
                            PB[_b][:, 0:128]), sS)
                        wait("tensor", sS, cp)
                barrier()

                t_to = dma("sync", sL, tern_o.rearrange("p (k e) -> p k e", k=8),
                           tqs[3].rearrange("(k p) e -> p k e", p=128))

                a2_u_done = 0
                a2_rne_done = 0
                prev_vzz = 0
                for j in range(8):
                    slab = arena[:, j * 2048:(j + 1) * 2048]
                    wait("scalar", sV, a2_u_done)
                    wait("tensor", sV, prev_vzz)
                    u_evac_S = [0, 0, 0, 0]
                    for m in range(16):
                        bank = 4 + (m % 4)
                        if m >= 4:
                            wait("tensor", sS, u_evac_S[m % 4])
                        tt = op("tensor", lambda e, _m=m, _b=bank, _s=slab: e.transpose(
                            PB[_b][:, 0:128], _s.rearrange("p (u s) -> p s u", s=16)[:, _m, :],
                            ident_f[:, :]), sT)
                        wait("scalar", sT, tt)
                        op("scalar", lambda e, _m=m, _b=bank: e.copy(
                            u0[:, _m * 64:(_m + 1) * 64], PB[_b][:, 0:64]), sS)
                        cp = op("scalar", lambda e, _m=m, _b=bank: e.copy(
                            u1[:, _m * 64:(_m + 1) * 64], PB[_b][:, 64:128]), sS)
                        u_evac_S[m % 4] = cp
                    wait("tensor", sS, sS.n)
                    if j == 0:
                        wait("tensor", sL, t_to)
                    for hsub, usrc in ((0, u0), (1, u1)):
                        hidx = 2 * j + hsub
                        trow = tok0 + hidx * TPB
                        if hsub == 1 or j > 0:
                            wait("sync", sV, sV.n)
                        t_x = dma("sync", sX, xres, x_in[trow:trow + TPB, :])
                        quant_chain(usrc, sq_gate=a2_rne_done if (hsub == 1 or j > 0) else None)
                        if hsub == 1:
                            a2_u_done = quant_chain.x_done
                        op("vector", lambda e: e.tensor_scalar(s_co, s_mxc, mw_b[:, 3:4], 1.0 / 127.0,
                                                               ALU.mult, ALU.mult))
                        v_kq = vgate()
                        a2_rne_done = v_kq
                        wait("tensor", sV, v_kq)
                        e_kqt = kq_transpose()
                        wait("tensor", sS, e_kqt)
                        for kc in range(8):
                            for eo in range(2):
                                op("tensor", lambda e, _k=kc, _e=eo: e.matmul(
                                    PB[4 + _e][:, :], kqTb[:, _k * 128:(_k + 1) * 128],
                                    tern_o[:, _k * 1024 + _e * 512:_k * 1024 + (_e + 1) * 512],
                                    start=(_k == 0), stop=(_k == 7)), sT)
                        t_mm = sT.n
                        wait("vector", sT, t_mm)
                        wait("vector", sL, t_x)
                        for eo in range(2):
                            op("vector", lambda e, _e=eo: e.scalar_tensor_tensor(
                                t0[:, _e * 512:(_e + 1) * 512], PB[4 + _e][:, :], s_co,
                                xres[:, _e * 512:(_e + 1) * 512], ALU.mult, ALU.add))
                        v_zz = op("vector", lambda e: e.engine_nop(), sV)
                        prev_vzz = v_zz
                        if debug and b == 0 and hidx == 0:
                            wait("gpsimd", sV, v_zz)
                            dma("gpsimd", sO, dbg["dbg_d"][:, :], t0)
                            wait("gpsimd", sO, sO.n)
                        v_x1 = ln_chain(t0, arena[:, hidx * 1024:(hidx + 1) * 1024])
                        wait("gpsimd", sV, v_x1)
                        dma("gpsimd", sPrep, x1d[hidx * TPB:(hidx + 1) * TPB, :],
                            arena[:, hidx * 1024:(hidx + 1) * 1024])
                barrier()

                if debug and b == 0:
                    dma("gpsimd", sO, dbg["dbg_c"][:, :], arena[:, 0:1024])
                    wait("gpsimd", sO, sO.n)
                cut(1)
                wait("sync", sGn, sGn.n)
                for t_ in range(TB):
                    x1t = arena[:, t_ * 1024:(t_ + 1) * 1024]
                    wait("tensor", sV, sV.n)
                    wait("tensor", sS, sS.n)
                    for g in range(2):
                        for kc4 in range(4):
                            kc = g * 4 + kc4
                            op("tensor", lambda e, _k=kc, _g=g, _c=kc4, _x=x1t: e.transpose(
                                PB[6 + _g][:, _c * 128:(_c + 1) * 128],
                                _x[:, _k * 128:(_k + 1) * 128], ident_f[:, :]), sT)
                    t_tp = sT.n
                    wait("scalar", sT, t_tp)
                    for g in range(2):
                        op("scalar", lambda e, _g=g: e.copy(
                            x1tT_h.rearrange("p (k c) -> p k c", k=8)[:, _g * 4:(_g + 1) * 4, :],
                            PB[6 + _g].rearrange("p (k c) -> p k c", k=4)), sS)
                    e_hi = sS.n
                    wait("vector", sS, e_hi)
                    for g in range(2):
                        op("vector", lambda e, _g=g: e.tensor_tensor(
                            x1tT_l.rearrange("p (k c) -> p k c", k=8)[:, _g * 4:(_g + 1) * 4, :],
                            PB[6 + _g].rearrange("p (k c) -> p k c", k=4),
                            x1tT_h.rearrange("p (k c) -> p k c", k=8)[:, _g * 4:(_g + 1) * 4, :],
                            ALU.subtract), sV)
                    v_lo = sV.n
                    wait("tensor", sV, v_lo)
                    for kc in range(8):
                        for pi, (lh, rh) in enumerate(((x1tT_h, wg_h), (x1tT_l, wg_h), (x1tT_h, wg_l))):
                            op("tensor", lambda e, _k=kc, _l=lh, _r=rh,
                               _fi=(kc == 0 and pi == 0), _la=(kc == 7 and pi == 2): e.matmul(
                                PB[3][:, 0:8],
                                _l[:, _k * 128:(_k + 1) * 128],
                                _r[:, _k * 8:(_k + 1) * 8], start=_fi, stop=_la), sT)
                    t_g = sT.n
                    wait("scalar", sT, t_g)
                    s_lz = op("scalar", lambda e: e.copy(lzb, PB[3][:, 0:8]), sS)
                    wait("vector", sS, s_lz)
                    v_nm2 = op("vector", lambda e: e.tensor_reduce(
                        s_nm, lzb, AX.X, ALU.max, negate=True), sV)
                    wait("scalar", sV, v_nm2)
                    s_zx = op("scalar", lambda e: e.activation(zxb, lzb, AF.Exp, bias=s_nm), sS)
                    wait("vector", sS, s_zx)
                    op("vector", lambda e: e.tensor_reduce(s_gm1, zxb, AX.X, ALU.max))
                    vgate()
                    op("vector", lambda e: e.tensor_scalar(eqb, zxb, s_gm1, None, ALU.is_equal))
                    vgate()
                    op("vector", lambda e: e.scalar_tensor_tensor(qqb, eqb, -4.0, zxb, ALU.mult, ALU.add))
                    vgate()
                    op("vector", lambda e: e.tensor_reduce(s_gm2, qqb, AX.X, ALU.max))
                    vgate()
                    op("vector", lambda e: e.tensor_scalar(selb, zxb, s_gm2, None, ALU.is_ge))
                    op("vector", lambda e: e.tensor_add(s_den, s_gm1, s_gm2))
                    vgate()
                    op("vector", lambda e: e.reciprocal(s_dinv, s_den))
                    op("vector", lambda e: e.tensor_mul(gsb, zxb, selb))
                    vgate()
                    op("vector", lambda e: e.tensor_tensor_scan(
                        sm16[:, 32:40], selb, selb, 0.0, ALU.add, ALU.bypass))
                    op("vector", lambda e, _t=t_: e.tensor_scalar(
                        gsc[:, 0:8], gsb, s_dinv, None, ALU.mult))
                    vgate()
                    op("vector", lambda e, _t=t_: e.tensor_copy(
                        pkall.rearrange("p (t e w) -> p t e w", t=16, w=2)[:, _t, :, 0:1],
                        gsc[:, 0:8].rearrange("p (e w) -> p e w", w=1)))
                    v_pk = op("vector", lambda e, _t=t_: e.tensor_copy(
                        pkall.rearrange("p (t e w) -> p t e w", t=16, w=2)[:, _t, :, 1:2],
                        sm16[:, 32:40].rearrange("p (e w) -> p e w", w=1)), sV)
                    wait("tensor", sV, v_pk)
                    t_sel = op("tensor", lambda e: e.transpose(
                        PB[2][0:8, 0:128], selb, ident_f[:, :]), sT)
                    wait("scalar", sT, t_sel)
                    cp_sel = op("scalar", lambda e, _t=t_: e.copy(
                        selT[:, _t * 128:(_t + 1) * 128], PB[2][0:8, 0:128]), sS)
                    wait("tensor", sS, cp_sel)

                cut(2)
                wait("vector", sS, sS.n)
                op("vector", lambda e: e.tensor_tensor_scan(
                    rankT, selT, selT, 0.0, ALU.add, ALU.bypass))
                vgate()
                op("vector", lambda e: e.tensor_sub(rankT, rankT, selT))
                vgate()
                op("vector", lambda e: e.tensor_scalar(
                    ltT, rankT, float(CAP), None, ALU.is_lt))
                vgate()
                op("vector", lambda e: e.tensor_mul(ltT, ltT, selT))
                op("vector", lambda e: e.tensor_scalar_add(slotT, rankT, -float(CAP)))
                vgate()
                op("vector", lambda e: e.tensor_mul(slotT, slotT, ltT))
                vgate()
                op("vector", lambda e: e.tensor_scalar_add(slotT, slotT, float(CAP)))
                vgate()
                v_slot = op("vector", lambda e: e.tensor_scalar(
                    slotT, slotT, 895.0, 0.0, ALU.min, ALU.max), sV)
                wait("tensor", sV, v_slot)
                prev_cp_sl = 0
                sc_done = [0, 0]
                gc_done = [0, 0]
                idx_pp = (s_idx, s_di)
                for t_ in range(TB):
                    pb_ = t_ % 2
                    if t_ > 0:
                        wait("tensor", sS, prev_cp_sl)
                    t_sl = op("tensor", lambda e, _t=t_: e.transpose(
                        PB[2][:, 0:8], slotT[:, _t * 128:(_t + 1) * 128], ident_f[0:8, 0:8]), sT)
                    wait("scalar", sT, t_sl)
                    wait("scalar", sG, gc_done[pb_])
                    cp_sl = op("scalar", lambda e, _p=pb_: e.copy(
                        sm16[:, 16 + _p * 8:24 + _p * 8], PB[2][:, 0:8]), sS)
                    prev_cp_sl = cp_sl
                    wait("gpsimd", sS, cp_sl)
                    wait("gpsimd", sSlot, sc_done[pb_])
                    gc = op("gpsimd", lambda e, _p=pb_: e.tensor_copy(
                        idx_pp[_p][:, 0:8], sm16[:, 16 + _p * 8:24 + _p * 8]), sG)
                    gc_done[pb_] = gc
                    wait("gpsimd", sG, gc)
                    if t_ == 0:
                        wait("gpsimd", sPrep, sPrep.n)
                    for e_ in range(NEXP):
                        idma("gpsimd", sSlot, idxe[e_][:, :], tokid_sb[:, t_:t_ + 1],
                             out_idx=idx_pp[pb_][:, e_:e_ + 1])
                    sc_done[pb_] = sSlot.n
                wait("sync", sV, sV.n)
                for e_ in range(NEXP):
                    dma("sync", sGsc, gscde[e_][0:2048, :].rearrange("(t p) c -> p t c", p=128),
                        pkall.rearrange("p (t e w) -> p t e w", t=16, w=2)[:, :, e_, :])
                gscde_done = sGsc.n

                cut(3)
                ygroups = ((0, 512), (512, 256))
                slot_last_T = [0, 0, 0]
                ybuf_scat_O = 0
                mac_done_V = 0
                for eexp in range(NEXP):
                    for g0, W_ in ygroups:
                        nst = W_ // 128
                        wait("sync", sSlot, sSlot.n)
                        wait("sync", sV, sV.n)
                        t_ix = dma("sync", sIdx, sm16[:, 8:8 + nst],
                            idxe[eexp][g0:g0 + W_, :].rearrange(
                            "(p s) o -> p (s o)", p=128))
                        wait("vector", sIdx, t_ix)
                        v_icl = op("vector", lambda e, _n=nst: e.tensor_scalar(
                            sm16[:, 8:8 + _n], sm16[:, 8:8 + _n], 2175.0, 0.0,
                            ALU.min, ALU.max), sV)
                        wait("gpsimd", sV, v_icl)
                        wait("gpsimd", sSlot, sSlot.n)
                        wait("gpsimd", sGx0, sGx0.n)
                        wait("gpsimd", sGx1, sGx1.n)
                        wait("gpsimd", sGn, sGn.n)
                        g_cast = op("gpsimd", lambda e, _n=nst: e.tensor_copy(
                            s_idx[:, 0:_n], sm16[:, 8:8 + _n]), sG)
                        wait("gpsimd", sG, g_cast)
                        prev_lo_V = 0
                        for st in range(nst):
                            xg = xgs[st % 2]
                            sGx = sGx0 if st % 2 == 0 else sGx1
                            if st >= 2:
                                wait("gpsimd", sT, sT.n)
                            ga = idma("gpsimd", sGx, xg, x1d[:, :],
                                      in_idx=s_idx[:, st:st + 1])
                            idma("gpsimd", sGn, sm16[:, st * 2:st * 2 + 2],
                                 gscde[eexp][:, :], in_idx=s_idx[:, st:st + 1])
                            wait("tensor", sGx, ga)
                            if st > 0:
                                wait("tensor", sS, sS.n)
                                wait("tensor", sV, prev_lo_V)
                            for g in range(2):
                                for kc4 in range(4):
                                    kc = g * 4 + kc4
                                    op("tensor", lambda e, _k=kc, _g=g, _c=kc4, _x=xg: e.transpose(
                                        PB[6 + _g][:, _c * 128:(_c + 1) * 128],
                                        _x[:, _k * 128:(_k + 1) * 128], ident_f[:, :]), sT)
                            t_tp = sT.n
                            wait("scalar", sT, t_tp)
                            for g in range(2):
                                op("scalar", lambda e, _g=g, _s=st: e.copy(
                                    x1t_h.rearrange("p (k c) -> p k c", k=8)[
                                        :, _g * 4:(_g + 1) * 4, _s * 128:(_s + 1) * 128],
                                    PB[6 + _g].rearrange("p (k c) -> p k c", k=4)), sS)
                            e_hi = sS.n
                            wait("vector", sS, e_hi)
                            for g in range(2):
                                op("vector", lambda e, _g=g, _s=st: e.tensor_tensor(
                                    x1t_l.rearrange("p (k c) -> p k c", k=8)[
                                        :, _g * 4:(_g + 1) * 4, _s * 128:(_s + 1) * 128],
                                    PB[6 + _g].rearrange("p (k c) -> p k c", k=4),
                                    x1t_h.rearrange("p (k c) -> p k c", k=8)[
                                        :, _g * 4:(_g + 1) * 4, _s * 128:(_s + 1) * 128],
                                    ALU.subtract), sV)
                            prev_lo_V = sV.n
                        v_xgT = op("vector", lambda e: e.engine_nop(), sV)
                        cut(3.2)
                        load_t = [None] * 8

                        def emit_stage_load(stg):
                            slot = wb[:, (stg % 3) * 8192:((stg % 3) + 1) * 8192]
                            s_hi = slot[:, 0:4096]
                            s_lo = slot[:, 4096:8192]
                            sws = sWS[stg % 3]
                            wait("sync", sT, slot_last_T[stg % 3])
                            if stg < 4:
                                csl = slice(stg * 512, (stg + 1) * 512)
                                dma("sync", sws, s_hi.rearrange("p (k c) -> p k c", k=8),
                                    w1s[eexp, 0, :, csl].rearrange("(k p) c -> p k c", p=128))
                                load_t[stg] = dma("sync", sws,
                                    s_lo.rearrange("p (k c) -> p k c", k=8),
                                    w1s[eexp, 1, :, csl].rearrange("(k p) c -> p k c", p=128))
                            else:
                                rsl = slice((stg - 4) * 512, (stg - 3) * 512)
                                dma("sync", sws, s_hi.rearrange("p (k c) -> p k c", k=4),
                                    w2s[eexp, 0, rsl, :].rearrange("(k p) c -> p k c", p=128))
                                load_t[stg] = dma("sync", sws,
                                    s_lo.rearrange("p (k c) -> p k c", k=4),
                                    w2s[eexp, 1, rsl, :].rearrange("(k p) c -> p k c", p=128))

                        for stg in range(3):
                            emit_stage_load(stg)
                        bank_evac_S = [0, 0]
                        h32_split_V = [0, 0]
                        for stg in range(8):
                            slot = wb[:, (stg % 3) * 8192:((stg % 3) + 1) * 8192]
                            s_hi = slot[:, 0:4096]
                            s_lo = slot[:, 4096:8192]
                            wait("tensor", sWS[stg % 3], load_t[stg])
                            if stg == 0:
                                wait("tensor", sV, max(v_xgT, mac_done_V))
                            if stg == 4:
                                wait("tensor", sS, max(bank_evac_S))
                                wait("tensor", sV, max(h32_split_V))
                            if stg < 4:
                                for hcL in range(4):
                                    hc = stg * 4 + hcL
                                    bank = hc % 2
                                    wait("tensor", sS, bank_evac_S[bank])
                                    for kc in range(8):
                                        for pi, (wsl_, xsl_) in enumerate(
                                                ((s_hi, x1t_h), (s_hi, x1t_l), (s_lo, x1t_h))):
                                            op("tensor", lambda e, _k=kc, _h=hcL, _w=wsl_,
                                               _x=xsl_, _b=bank, _W=W_,
                                               _fi=(kc == 0 and pi == 0),
                                               _la=(kc == 7 and pi == 2): e.matmul(
                                                PB[_b][:, 0:_W],
                                                _w[:, _k * 512 + _h * 128:_k * 512 + (_h + 1) * 128],
                                                _x[:, _k * 512:_k * 512 + _W],
                                                start=_fi, stop=_la), sT)
                                    t_h = sT.n
                                    wait("scalar", sT, t_h)
                                    wait("scalar", sV, h32_split_V[hc % 2])
                                    ss_h = op("scalar", lambda e, _hc=hc, _b=bank, _e=eexp,
                                              _W=W_: e.activation(
                                        h32s[_hc % 2][:, 0:_W], PB[_b][:, 0:_W], AF.Relu,
                                        bias=b1_sb[:, _e * 16 + _hc:_e * 16 + _hc + 1]), sS)
                                    bank_evac_S[bank] = ss_h
                                    wait("vector", sS, ss_h)
                                    op("vector", lambda e, _hc=hc, _W=W_: e.tensor_copy(
                                        h_hi[:, _hc * 512:_hc * 512 + _W],
                                        h32s[_hc % 2][:, 0:_W]))
                                    v_hl = op("vector", lambda e, _hc=hc, _W=W_: e.tensor_sub(
                                        h_lo[:, _hc * 512:_hc * 512 + _W],
                                        h32s[_hc % 2][:, 0:_W],
                                        h_hi[:, _hc * 512:_hc * 512 + _W]), sV)
                                    h32_split_V[hc % 2] = v_hl
                            else:
                                for dc in range(8):
                                    for kcb in range(4):
                                        hcb = (stg - 4) * 4 + kcb
                                        for pi, (wsl_, hsl_) in enumerate(
                                                ((s_hi, h_hi), (s_hi, h_lo), (s_lo, h_hi))):
                                            op("tensor", lambda e, _k=kcb, _d=dc, _w=wsl_,
                                               _x=hsl_, _hcb=hcb, _W=W_,
                                               _fi=(stg == 4 and kcb == 0 and pi == 0),
                                               _la=(stg == 7 and kcb == 3 and pi == 2):
                                               e.matmul(
                                                PB[_d][:, 0:_W],
                                                _w[:, _k * 1024 + _d * 128:_k * 1024 + (_d + 1) * 128],
                                                _x[:, _hcb * 512:_hcb * 512 + _W],
                                                start=_fi, stop=_la), sT)
                            slot_last_T[stg % 3] = sT.n
                            if stg + 3 <= 7:
                                emit_stage_load(stg + 3)
                        cut(3.5)
                        cut(3.7)
                        t_y = sT.n
                        wait("scalar", sT, t_y)
                        wait("vector", sGn, sGn.n)
                        wait("vector", sSc, ybuf_scat_O)
                        gnw = fpool[:, 1024:1024 + nst * 128]
                        for st in range(nst):
                            op("vector", lambda e, _s=st, _gw=gnw: e.tensor_scalar(
                                _gw[:, _s * 128:(_s + 1) * 128], zsrc[:, 0:128],
                                sm16[:, _s * 2:_s * 2 + 1], None, ALU.add))
                        v_gnw = op("vector", lambda e: e.engine_nop(), sV)
                        y_tp_T = [0, 0]
                        for dc in range(8):
                            wait("scalar", sT, y_tp_T[dc % 2])
                            ss_y = op("scalar", lambda e, _d=dc, _e=eexp, _W=W_: e.activation(
                                y_sl[_d % 2][:, 0:_W], PB[_d][:, 0:_W], AF.Identity,
                                bias=b2_sb[:, _e * 8 + _d:_e * 8 + _d + 1]), sS)
                            if stop is not None and stop <= 3.75:
                                continue
                            wait("tensor", sS, ss_y)
                            for st in range(nst):
                                op("tensor", lambda e, _s=st, _d=dc: e.transpose(
                                    PB[_d][:, _s * 128:(_s + 1) * 128],
                                    y_sl[_d % 2][:, _s * 128:(_s + 1) * 128],
                                    ident_f[:, :]), sT)
                            t_all = sT.n
                            y_tp_T[dc % 2] = t_all
                            if stop is not None and stop <= 3.8:
                                continue
                            wait("vector", sT, t_all)
                            if dc == 0:
                                wait("vector", sV, v_gnw)
                            op("vector", lambda e, _d=dc, _n=nst, _gw=gnw: e.tensor_tensor(
                                ybuf.rearrange("p (s q) -> p s q", q=1024)[
                                    :, 0:_n, _d * 128:(_d + 1) * 128],
                                PB[_d][:, 0:_n * 128].rearrange("p (s c) -> p s c", c=128),
                                _gw[:, 0:_n * 128].rearrange("p (s c) -> p s c", c=128),
                                ALU.mult), sV)
                        mac_done_V = sV.n
                        cut(3.79)
                        cut(3.85)
                        cut(3.9)
                        for st in range(nst):
                            op("vector", lambda e, _s=st: e.scalar_tensor_tensor(
                                sm16[:, 24 + _s:25 + _s], sm16[:, _s * 2 + 1:_s * 2 + 2],
                                YOFF, sm16[:, 8 + _s:9 + _s], ALU.mult, ALU.add))
                        vgate()
                        op("vector", lambda e, _n=nst: e.tensor_scalar_add(
                            sm16[:, 24:24 + _n], sm16[:, 24:24 + _n], -YOFF))
                        vgate()
                        v_di = op("vector", lambda e, _n=nst: e.tensor_scalar(
                            sm16[:, 24:24 + _n], sm16[:, 24:24 + _n], 8319.0, 0.0,
                            ALU.min, ALU.max), sV)
                        wait("gpsimd", sV, max(v_di, mac_done_V))
                        wait("gpsimd", sSc, sSc.n)
                        g_cd = op("gpsimd", lambda e, _n=nst: e.tensor_copy(
                            s_di[:, 0:_n], sm16[:, 24:24 + _n]), sG)
                        wait("gpsimd", sG, g_cd)
                        wait("gpsimd", sPrep, sPrep.n)
                        for st in range(nst):
                            idma("gpsimd", sSc, ystage[:, :],
                                 ybuf[:, st * 1024:(st + 1) * 1024],
                                 out_idx=s_di[:, st:st + 1])
                        ybuf_scat_O = sSc.n
                        cut(4 + eexp * 2 + (1 if g0 else 0))

                wait("sync", sSc, sSc.n)
                s2_ld = [0] * TB
                s2_cons = [0] * TB
                s2buf = (fpool[:, 4096:6144], fpool[:, 2048:4096])
                for t_ in range(TB + 1):
                    if t_ < TB:
                        sb2 = s2buf[t_ % 2]
                        if t_ >= 2:
                            wait("sync", sV, s2_cons[t_ - 2])
                        dma("sync", sY2, sb2[:, 0:1024],
                            ystage[t_ * 128:(t_ + 1) * 128, :])
                        s2_ld[t_] = dma("sync", sY2, sb2[:, 1024:2048],
                                        ystage[4160 + t_ * 128:4160 + (t_ + 1) * 128, :])
                    if t_ == 0:
                        continue
                    tc = t_ - 1
                    sb2c = s2buf[tc % 2]
                    wait("vector", sY2, s2_ld[tc])
                    wait("vector", sOut, sOut.n)
                    s2_cons[tc] = op("vector", lambda e, _b=sb2c: e.tensor_add(
                        t0, _b[:, 0:1024], _b[:, 1024:2048]), sV)
                    vgate()
                    op("vector", lambda e, _t=tc: e.tensor_add(
                        t0, t0, arena[:, _t * 1024:(_t + 1) * 1024]))
                    v_o = ln_chain(t0, t1)
                    wait("gpsimd", sV, v_o)
                    trow = tok0 + tc * TPB
                    dma("gpsimd", sOut, y_out[trow:trow + TPB, :], t1)
                barrier()


        except _CutDone:
            pass
        wait("gpsimd", sO, sO.n)
        wait("gpsimd", sOut, sOut.n)
        wait("gpsimd", sSc, sSc.n)
        wait("gpsimd", sPrep, sPrep.n)
        wait("gpsimd", sSlot, sSlot.n)

        block = es.enter_context(nc.Block())

        @block.sync
        def _(eng):
            for fn in prog["sync"]:
                fn(eng)

        @block.vector
        def _(eng):
            for fn in prog["vector"]:
                fn(eng)

        @block.tensor
        def _(eng):
            for fn in prog["tensor"]:
                fn(eng)

        @block.scalar
        def _(eng):
            for fn in prog["scalar"]:
                fn(eng)

        @block.gpsimd
        def _(eng):
            for fn in prog["gpsimd"]:
                fn(eng)

    return nc


def kernel(L0_attn_x, L0_attn_wq, L0_attn_wk, L0_attn_wv, L0_attn_wo,
           L0_ffn_Wg, L0_ffn_W1, L0_ffn_b1, L0_ffn_W2, L0_ffn_b2, _debug=False):
    x = np.ascontiguousarray(np.asarray(L0_attn_x, dtype=np.float32))
    assert x.shape == (16, 2048, 1024)

    wqT = np.ascontiguousarray(np.asarray(L0_attn_wq, np.float32).T)
    wkT = np.ascontiguousarray(np.asarray(L0_attn_wk, np.float32).T)
    wvT = np.ascontiguousarray(np.asarray(L0_attn_wv, np.float32).T)
    woT = np.ascontiguousarray(np.asarray(L0_attn_wo, np.float32).T)
    w1 = np.ascontiguousarray(np.asarray(L0_ffn_W1, np.float32))
    w2 = np.ascontiguousarray(np.asarray(L0_ffn_W2, np.float32))
    b1r = np.ascontiguousarray(
        np.asarray(L0_ffn_b1, np.float32).reshape(8, 16, 128).transpose(2, 0, 1).reshape(128, 128))
    b2r = np.ascontiguousarray(
        np.asarray(L0_ffn_b2, np.float32).reshape(8, 8, 128).transpose(2, 0, 1).reshape(128, 64))
    wgr = np.ascontiguousarray(
        np.asarray(L0_ffn_Wg, np.float32).reshape(8, 128, 8).transpose(1, 0, 2).reshape(128, 64))
    identf = np.eye(128, dtype=np.float32)
    tokids = np.zeros((128, 18), dtype=np.float32)
    for t in range(16):
        tokids[:, t] = t * 128 + np.arange(128)
    tokids[:, 16] = (np.arange(128) % 8) * CAP

    nc = build_program(debug=_debug)
    in_maps = []
    for c in range(NCORE):
        xc = np.ascontiguousarray(x[2 * c:2 * c + 2].reshape(4096, 1024))
        in_maps.append({
            "x": xc, "wqT": wqT, "wkT": wkT, "wvT": wvT, "woT": woT,
            "wgr": wgr, "w1": w1, "b1r": b1r, "w2": w2, "b2r": b2r,
            "identf": identf, "tokids": tokids,
        })
    res = run_bass_kernel_spmd(nc, in_maps, core_ids=list(range(NCORE)))
    out = np.empty((16, 2048, 1024), dtype=np.float32)
    for c in range(NCORE):
        out[2 * c:2 * c + 2] = np.asarray(res.results[c]["y"]).reshape(2, 2048, 1024)
    if _debug:
        return out, res
    return out
